# version 07 (final)
# baseline (speedup 1.0000x reference)
import math
import numpy as np
import ml_dtypes
import concourse.bass as bass
import concourse.mybir as mybir
from concourse.bass_utils import run_bass_kernel_spmd

F32 = mybir.dt.float32
BF16 = mybir.dt.bfloat16
AF = mybir.ActivationFunctionType
ALU = mybir.AluOpType
AXX = mybir.AxisListType.X

B, T, D = 2, 4096, 4096
NST = 16
NH, NKV, DH = 32, 8, 128
NE, NR = 4, 16
ER = NE * NR
TOKS = 1024
DSL = 512
SHIFT = 12.0
SCALE_QK = 1.0 / math.sqrt(DH)
LORA_SCALE = 4.0
EPS = 1e-5
NEG = -1.0e9
KVQ_ROWS = NKV * DH + TOKS + NH * DH


def _chunks_of_core(cid):
    q = cid % 4
    return (q, 7 - q)


def _tok_positions(cid):
    lo, hi = _chunks_of_core(cid)
    return np.concatenate([np.arange(lo * 512, lo * 512 + 512),
                           np.arange(hi * 512, hi * 512 + 512)])


def _llama3_cos_sin(positions):
    ROPE_THETA, ROPE_FACTOR, HIGH_FF, LOW_FF, ORIG = 500000.0, 8.0, 4.0, 1.0, 8192
    inv = 1.0 / ROPE_THETA ** (np.arange(0, DH, 2, dtype=np.float32) / DH)
    wavelen = 2.0 * np.pi / inv
    inv_scaled = np.where(wavelen > ORIG / LOW_FF, inv / ROPE_FACTOR, inv)
    smooth = (ORIG / wavelen - LOW_FF) / (HIGH_FF - LOW_FF)
    smoothed = (1.0 - smooth) / ROPE_FACTOR * inv + smooth * inv
    is_med = (wavelen <= ORIG / LOW_FF) & (wavelen >= ORIG / HIGH_FF)
    invf = np.where(is_med, smoothed, inv_scaled).astype(np.float32)
    freqs = positions.astype(np.float32)[None, :] * invf[:, None]
    emb = np.concatenate([freqs, freqs], axis=0)
    return np.cos(emb).astype(np.float32), np.sin(emb).astype(np.float32)


class Ctx:

    NPOOL = 96

    def __init__(self, nc):
        self.nc = nc
        self.sems = [nc.alloc_semaphore(f"sp{i}") for i in range(self.NPOOL)]
        self.scnt = [0] * self.NPOOL
        self.free_cmp = list(range(40))
        self.free_dma = list(range(40, 70))
        self.free_sw = list(range(70, self.NPOOL))
        self.kind = {}
        self.name2idx = {}
        self.base = {}
        self.history = []

    def _get(self, name, kind=0):
        if name not in self.name2idx:
            pool = (self.free_cmp, self.free_dma, self.free_sw)[kind]
            if not pool:
                raise RuntimeError("sem pool exhausted; add a barrier")
            idx = pool.pop(0)
            self.name2idx[name] = idx
            self.base[name] = self.scnt[idx]
            self.kind[name] = kind
            self.history.append((name, idx))
        return self.name2idx[name]

    def retire_all(self):
        for name, idx in self.name2idx.items():
            (self.free_cmp, self.free_dma, self.free_sw)[self.kind[name]].append(idx)
        self.name2idx = {}
        self.base = {}
        self.kind = {}

    def inc(self, name, inst, n=1):
        idx = self._get(name)
        self.scnt[idx] += n
        inst.then_inc(self.sems[idx], n)
        return self.scnt[idx] - self.base[name]

    def n(self, name):
        idx = self._get(name)
        return self.scnt[idx] - self.base[name]

    @staticmethod
    def _qkind(eng):
        return 2 if eng.engine == mybir.EngineType.Pool else 1

    def dma(self, eng, qname, out, in_):
        self._get(qname, kind=self._qkind(eng))
        inst = eng.dma_start(out=out, in_=in_)
        return self.inc(qname, inst, 16)

    def dmap(self, eng, base, idx, nbuf, out, in_):
        self._get(base + str(idx % nbuf), kind=self._qkind(eng))
        inst = eng.dma_start(out=out, in_=in_)
        return self.inc(base + str(idx % nbuf), inst, 16)

    def wdmap(self, eng, base, idx, nbuf, per_idx=1):
        if idx < 0:
            return
        self.wait(eng, base + str(idx % nbuf), (idx // nbuf + 1) * 16 * per_idx)

    def wait(self, eng, name, val):
        if val > 0:
            idx = self._get(name)
            eng.wait_ge(self.sems[idx], self.base[name] + val)

    def ser(self, eng, name, inst):
        v = self.inc(name, inst)
        idx = self.name2idx[name]
        eng.wait_ge(self.sems[idx], self.scnt[idx])
        return v


def build_program(dbg=()):
    nc = bass.Bass("TRN2", target_bir_lowering=False)
    c = Ctx(nc)

    x_rows = nc.dram_tensor("x_rows", [TOKS, D], F32, kind="ExternalInput").ap()
    out_rows = nc.dram_tensor("out_rows", [TOKS, D], F32, kind="ExternalOutput").ap()
    wqT = nc.dram_tensor("wqT", [D, NH * DH], F32, kind="ExternalInput").ap()
    wkT = nc.dram_tensor("wkT", [D, NKV * DH], F32, kind="ExternalInput").ap()
    wvT = nc.dram_tensor("wvT", [D, NKV * DH], F32, kind="ExternalInput").ap()
    woT = nc.dram_tensor("woT", [D, D], F32, kind="ExternalInput").ap()
    xpwT = nc.dram_tensor("xpwT", [D, 160], F32, kind="ExternalInput").ap()
    dtwT = nc.dram_tensor("dtwT", [128, D], F32, kind="ExternalInput").ap()
    dtbv = nc.dram_tensor("dtbv", [D, 1], F32, kind="ExternalInput").ap()
    a_log = nc.dram_tensor("a_log", [D, NST], F32, kind="ExternalInput").ap()
    dvec = nc.dram_tensor("dvec", [D, 1], F32, kind="ExternalInput").ap()
    routT = nc.dram_tensor("routT", [D, NE], F32, kind="ExternalInput").ap()
    ffnAT = nc.dram_tensor("ffnAT", [D, ER], F32, kind="ExternalInput").ap()
    ffnB = nc.dram_tensor("ffnB", [ER, D], F32, kind="ExternalInput").ap()
    identd = nc.dram_tensor("identd", [128, 128], F32, kind="ExternalInput").ap()
    identbd = nc.dram_tensor("identbd", [128, 128], F32, kind="ExternalInput").ap()
    masksd = nc.dram_tensor("masksd", [128, 896], F32, kind="ExternalInput").ap()
    cosd = nc.dram_tensor("cosd", [128, TOKS], F32, kind="ExternalInput").ap()
    sind = nc.dram_tensor("sind", [128, TOKS], F32, kind="ExternalInput").ap()
    onesd = nc.dram_tensor("onesd", [128, 128], F32, kind="ExternalInput").ap()
    sel16d = nc.dram_tensor("sel16d", [NE, ER], F32, kind="ExternalInput").ap()
    constsd = nc.dram_tensor("constsd", [128, 8], F32, kind="ExternalInput").ap()

    xT_dram = nc.dram_tensor("xT_dram", [D, TOKS], F32).ap()
    xn1_loc = nc.dram_tensor("xn1_loc", [D, TOKS], BF16).ap()
    xn1_ag = nc.dram_tensor("xn1_ag", [4 * D, TOKS], BF16).ap()
    xp_loc = nc.dram_tensor("xp_loc", [160, TOKS], BF16).ap()
    xp_ag = nc.dram_tensor("xp_ag", [4 * 160, TOKS], BF16).ap()
    y_loc = nc.dram_tensor("y_loc", [1024, T], BF16).ap()
    y_ag = nc.dram_tensor("y_ag", [D, T], BF16).ap()
    bc_mine = nc.dram_tensor("bc_mine", [2 * NST, T], BF16).ap()
    kvq_loc = nc.dram_tensor("kvq_loc", [KVQ_ROWS, TOKS], BF16).ap()
    kvq_ag = nc.dram_tensor("kvq_ag", [4 * KVQ_ROWS, TOKS], BF16).ap()
    attn_loc = nc.dram_tensor("attn_loc", [8 * DH, T], BF16).ap()
    attn_ag = nc.dram_tensor("attn_ag", [4 * 8 * DH, T], BF16).ap()
    xn1_mine = nc.dram_tensor("xn1_mine", [4 * 1024, TOKS], BF16).ap()
    y_mine = nc.dram_tensor("y_mine", [D, TOKS], BF16).ap()
    q_mine = nc.dram_tensor("q_mine", [4 * 1024, TOKS], BF16).ap()
    k_mine = nc.dram_tensor("k_mine", [4 * 256, TOKS], BF16).ap()
    v_mine = nc.dram_tensor("v_mine", [4 * 256, TOKS], BF16).ap()
    attn_mine = nc.dram_tensor("attn_mine", [NH * DH, TOKS], BF16).ap()
    alog_mine = nc.dram_tensor("alog_mine", [1024, NST], F32).ap()
    dtb_mine = nc.dram_tensor("dtb_mine", [1024, 1], F32).ap()
    dvec_mine = nc.dram_tensor("dvec_mine", [1024, 1], F32).ap()
    dtw_mine = nc.dram_tensor("dtw_mine", [128, 1024], F32).ap()

    dbg_specs = {
        "xT_dram": (xT_dram, [D, TOKS], F32),
        "xn1_loc": (xn1_loc, [D, TOKS], BF16),
        "xp_loc": (xp_loc, [160, TOKS], BF16),
        "xp_ag": (xp_ag, [4 * 160, TOKS], BF16),
        "y_loc": (y_loc, [1024, T], BF16),
        "x1": (xT_dram, [D, TOKS], F32),
        "x2": (xT_dram, [D, TOKS], F32),
        "x3": (xT_dram, [D, TOKS], F32),
        "x4": (xT_dram, [D, TOKS], F32),
        "kvq_loc": (kvq_loc, [KVQ_ROWS, TOKS], BF16),
        "q_mine": (q_mine, [4 * 1024, TOKS], BF16),
        "attn_loc": (attn_loc, [8 * DH, T], BF16),
    }
    dbg_aps = {}
    for nm in dbg:
        src, shp, dt_ = dbg_specs[nm]
        dbg_aps[nm] = (nc.dram_tensor("dbg_" + nm, shp, dt_, kind="ExternalOutput").ap(), src)

    PF = nc.alloc_sbuf_tensor("PF", [128, 6 * 4096], F32)
    PB = nc.alloc_sbuf_tensor("PB", [128, 10 * 4096], BF16)
    PB2 = nc.alloc_sbuf_tensor("PB2", [128, 8192], BF16)
    PS = nc.alloc_sbuf_tensor("PS", [128, 2048], F32)
    WS = nc.alloc_sbuf_tensor("WS", [128, 2560], BF16)
    identf_sb = nc.alloc_sbuf_tensor("identf_sb", [128, 128], F32)
    identb_sb = nc.alloc_sbuf_tensor("identb_sb", [128, 128], BF16)
    ones_bf = nc.alloc_sbuf_tensor("ones_bf", [128, 128], BF16)
    ones_f = nc.alloc_sbuf_tensor("ones_f", [128, 128], F32)
    sel16_sb = nc.alloc_sbuf_tensor("sel16_sb", [NE, ER], BF16)
    CONSTS = nc.alloc_sbuf_tensor("CONSTS", [128, 8], F32)

    PSUM = [nc.alloc_psum_tensor(f"psb{i}", [128, 512], F32) for i in range(8)]
    cos_sb = PB2[:, 4096:5120]
    sin_sb = PB2[:, 5120:6144]
    masks_sb = PB2[:, 6144:7040]

    def pf(i, w=4096):
        return PF[:, i * 4096:i * 4096 + w]

    def pb(i, w=4096):
        return PB[:, i * 4096:i * 4096 + w]

    blk = nc.Block()
    blk.__enter__()

    def barrier():
        snap = [(c.sems[idx], c.scnt[idx]) for idx in sorted(set(c.name2idx.values()))]
        for get in (blk.tensor, blk.scalar, blk.vector, blk.gpsimd, blk.sync):
            def body(eng, snap=snap):
                for s, v in snap:
                    if v > 0:
                        eng.wait_ge(s, v)
            get(body)
        c.retire_all()

    def emit_dbg(nm):
        if nm not in dbg_aps:
            return
        dst, src = dbg_aps[nm]
        barrier()

        def go(sy):
            rows = src.shape[0]
            r0 = 0
            while r0 < rows:
                rr = min(128, rows - r0)
                c.dma(sy, "qsync", dst[r0:r0 + rr, :], src[r0:r0 + rr, :])
                r0 += rr
        blk.sync(go)
        barrier()

    def p_consts(sy):
        c.dma(sy, "qsync", CONSTS[:], constsd)
        c.dma(sy, "qsync", identf_sb[:], identd)
        c.dma(sy, "qsync", ones_f[:], onesd)
        

    blk.sync(p_consts)

    def p_stage0(a):
        pid = a.partition_id()
        q = pid % 4
        c.dma(a, "qs", alog_mine, a_log[bass.ts(q, 1024), :])
        c.dma(a, "qs", dtb_mine, dtbv[bass.ts(q, 1024), :])
        c.dma(a, "qs", dvec_mine, dvec[bass.ts(q, 1024), :])
        c.dma(a, "qs", dtw_mine, dtwT[:, bass.ts(q, 1024)])
    blk.scalar(p_stage0)

    def p_consts_g(g):
        c.dma(g, "qg", ones_bf[:], onesd)
        c.dma(g, "qg", sel16_sb[:], sel16d)
        c.dma(g, "qg", identb_sb[:], identbd)
    blk.gpsimd(p_consts_g)
    barrier()

    def emit_stats_mm(t, tag, dg, xb_ap, sq_ap, last=31):
        for q2, src in ((0, xb_ap), (1, sq_ap)):
            for ntile in range(2):
                inst = t.matmul(PSUM[4 + 2 * q2 + ntile][0:1, :], ones_bf[:, 0:1],
                                src[:, ntile * 512:(ntile + 1) * 512],
                                start=(dg == 0), stop=(dg == last))
                c.inc(tag + "_mm", inst)

    def emit_ln_finalize(tag, stats_tag):
        def v_fin(v):
            c.wait(v, stats_tag + "_mm", c.n(stats_tag + "_mm"))
            v.tensor_scalar_mul(PS[0:1, 0:512], PSUM[4][0:1, :], 1.0 / D)
            v.tensor_scalar_mul(PS[0:1, 512:1024], PSUM[5][0:1, :], 1.0 / D)
            v.tensor_scalar_mul(PS[0:1, 1024:1536], PSUM[6][0:1, :], 1.0 / D)
            inst = v.tensor_scalar_mul(PS[0:1, 1536:2048], PSUM[7][0:1, :], 1.0 / D)
            c.ser(v, tag + "_ch", inst)
            v.tensor_mul(PSUM[4][0:1, :], PS[0:1, 0:512], PS[0:1, 0:512])
            inst = v.tensor_mul(PSUM[5][0:1, :], PS[0:1, 512:1024], PS[0:1, 512:1024])
            c.ser(v, tag + "_ch", inst)
            v.tensor_sub(PS[0:1, 1024:1536], PS[0:1, 1024:1536], PSUM[4][0:1, :])
            inst = v.tensor_sub(PS[0:1, 1536:2048], PS[0:1, 1536:2048], PSUM[5][0:1, :])
            c.inc(tag + "_var", inst)
        blk.vector(v_fin)

        def a_fin(a):
            c.wait(a, tag + "_var", c.n(tag + "_var"))
            inst = a.activation(out=PS[0:1, 1024:2048], in_=PS[0:1, 1024:2048], func=AF.Ln,
                                bias=CONSTS[0:1, 0:1])
            c.ser(a, tag + "_cha", inst)
            inst = a.activation(out=PS[0:1, 1024:2048], in_=PS[0:1, 1024:2048], func=AF.Exp,
                                scale=CONSTS[0:1, 3:4])
            c.inc(tag + "_fin", inst)
        blk.scalar(a_fin)

        def t_bc(t):
            c.wait(t, tag + "_fin", c.n(tag + "_fin"))
            for ntile in range(2):
                inst = t.matmul(PSUM[4 + ntile][:, :], ones_f[0:1, :],
                                PS[0:1, ntile * 512:(ntile + 1) * 512], start=True, stop=True)
                c.inc(tag + "_bc", inst)
            for ntile in range(2):
                inst = t.matmul(PSUM[6 + ntile][:, :], ones_f[0:1, :],
                                PS[0:1, 1024 + ntile * 512:1024 + (ntile + 1) * 512],
                                start=True, stop=True)
                c.inc(tag + "_bc", inst)
        blk.tensor(t_bc)

    def emit_xn_tile(v, tag, lane_src, lane_tmp, xn_dst):
        for ntile in range(2):
            sl = slice(ntile * 512, (ntile + 1) * 512)
            inst = v.tensor_sub(lane_tmp[:, sl], lane_src[:, sl], PSUM[4 + ntile][:, :])
            c.ser(v, tag + "_s", inst)
            inst = v.tensor_mul(xn_dst[:, sl], lane_tmp[:, sl], PSUM[6 + ntile][:, :])
            c.inc(tag, inst)

    def p0_sync(sy):
        for dg2 in range(0, 16, 2):
            if dg2 >= 2:
                c.wait(sy, "p0_tr", (dg2 - 1) * 16)
            src = x_rows[:, dg2 * 256:(dg2 + 1) * 256].rearrange("(a p) w -> p a w", p=128)
            c.dmap(sy, "p0_ld", dg2, 2, pf(dg2 % 2, 2048), src)
    blk.sync(p0_sync)



    def p0_stats_for(t, dg2s):
        c.wait(t, "p0_cast", (dg2s + 1) * 4)
        for ds in range(2):
            off = (dg2s % 2) * 4096 + ds * 2048
            emit_stats_mm(t, "p0_st", dg2s * 2 + ds,
                          PB2[:, off:off + 1024], PB2[:, off + 1024:off + 2048], last=31)

    def p0_tensor(t):
        for dg2 in range(16):
            c.wdmap(t, "p0_ld", dg2, 2)
            for j in range(16):
                g = dg2 * 16 + j
                if g >= 4:
                    c.wait(t, "p0_ev", g - 3)
                tt, ds = j // 2, j % 2
                src = pf(dg2 % 2, 2048)[:, tt * 256 + ds * 128: tt * 256 + ds * 128 + 128]
                inst = t.transpose(PSUM[g % 4][:, 0:128], src, identf_sb[:])
                c.inc("p0_tr", inst)
            if dg2 >= 1:
                p0_stats_for(t, dg2 - 1)
        p0_stats_for(t, 15)
    blk.tensor(p0_tensor)

    def p0_vector(v):
        for dg2 in range(16):
            stage = [pf(2 + 2 * (dg2 % 2), 1024), pf(3 + 2 * (dg2 % 2), 1024)]
            if dg2 >= 2:
                c.wait(v, "p0_cast", (dg2 - 2) * 4 + 4)
                c.wdmap(v, "p0_xst", dg2 - 2, 2, per_idx=2)
            for j in range(16):
                g = dg2 * 16 + j
                c.wait(v, "p0_tr", g + 1)
                tt, ds = j // 2, j % 2
                inst = v.tensor_copy(stage[ds][:, tt * 128:(tt + 1) * 128], PSUM[g % 4][:, 0:128])
                c.inc("p0_ev", inst)
    blk.vector(p0_vector)

    def p0_scalar(a):
        def oddload(d2):
            if d2 < 16:
                if d2 >= 2:
                    c.wait(a, "p0_tr", (d2 - 1) * 16)
                srcx = x_rows[:, d2 * 256:(d2 + 1) * 256].rearrange("(a p) w -> p a w", p=128)
                c.dmap(a, "p0_ld", d2, 2, pf(d2 % 2, 2048), srcx)
        oddload(1)
        for dg2 in range(16):
            if dg2 % 2 == 0:
                oddload(dg2 + 3)
            stage = [pf(2 + 2 * (dg2 % 2), 1024), pf(3 + 2 * (dg2 % 2), 1024)]
            c.wait(a, "p0_st_mm", max(0, (dg2 - 1) * 8))
            c.wait(a, "p0_ev", dg2 * 16 + 16)
            for ds in range(2):
                off = (dg2 % 2) * 4096 + ds * 2048
                inst = a.activation(out=PB2[:, off:off + 1024], in_=stage[ds], func=AF.Identity)
                c.inc("p0_cast", inst)
                inst = a.activation(out=PB2[:, off + 1024:off + 2048], in_=stage[ds], func=AF.Square)
                c.inc("p0_cast", inst)
                dg = dg2 * 2 + ds
                c.dmap(a, "p0_xst", dg2, 2, xT_dram[dg * 128:(dg + 1) * 128, :], stage[ds])
    blk.scalar(p0_scalar)

    emit_ln_finalize("ln1", "p0_st")
    barrier()

    def p0b_sync(sy):
        for dg in range(32):
            if dg >= 2:
                c.wait(sy, "p0b_xn", (dg - 2) * 2 + 2)
            c.dmap(sy, "p0b_ld", dg, 2, pf(dg % 2, 1024), xT_dram[dg * 128:(dg + 1) * 128, :])
    blk.sync(p0b_sync)

    def p0b_g(g):
        for dg in range(32):
            if dg >= 2:
                c.wait(g, "p0b_mm", (dg - 1) * 4)
            off = (dg % 2) * 256
            c.dmap(g, "p0b_w", dg, 2, WS[:, off:off + 160], xpwT[dg * 128:(dg + 1) * 128, :])
    blk.gpsimd(p0b_g)

    def p0b_vector(v):
        for dg in range(32):
            c.wdmap(v, "p0b_ld", dg, 2)
            if dg >= 2:
                c.wdmap(v, "p0b_xnst", dg - 2, 2)
                c.wait(v, "p0b_mm", (dg - 1) * 4)
            emit_xn_tile(v, "p0b_xn", pf(dg % 2, 1024), pf(2, 1024), pb(dg % 2, 1024))
    blk.vector(p0b_vector)

    def p0b_scalar(a):
        for dg in range(32):
            c.wait(a, "p0b_xn", dg * 2 + 2)
            c.dmap(a, "p0b_xnst", dg, 2, xn1_loc[dg * 128:(dg + 1) * 128, :], pb(dg % 2, 1024))
    blk.scalar(p0b_scalar)

    def p0b_tensor(t):
        for dg in range(32):
            c.wdmap(t, "p0b_w", dg, 2)
            c.wait(t, "p0b_xn", dg * 2 + 2)
            off = (dg % 2) * 256
            lhs1 = WS[:, off:off + 128]
            lhs2 = WS[:, off + 128:off + 160]
            xn = pb(dg % 2, 1024)
            for ntile in range(2):
                inst = t.matmul(PSUM[ntile][:, :], lhs1, xn[:, ntile * 512:(ntile + 1) * 512],
                                start=(dg == 0), stop=(dg == 31))
                c.inc("p0b_mm", inst)
            for ntile in range(2):
                inst = t.matmul(PSUM[2 + ntile][0:32, :], lhs2, xn[:, ntile * 512:(ntile + 1) * 512],
                                start=(dg == 0), stop=(dg == 31))
                c.inc("p0b_mm", inst)
    blk.tensor(p0b_tensor)

    def p0b_scalar2(a):
        c.wait(a, "p0b_mm", c.n("p0b_mm"))
        for ntile in range(2):
            inst = a.activation(out=pb(2, 1024)[:, ntile * 512:(ntile + 1) * 512],
                                in_=PSUM[ntile][:, :], func=AF.Identity)
            c.inc("p0b_ev", inst)
        for ntile in range(2):
            inst = a.activation(out=pb(3, 1024)[0:32, ntile * 512:(ntile + 1) * 512],
                                in_=PSUM[2 + ntile][0:32, :], func=AF.Identity)
            c.inc("p0b_ev", inst)
        c.wait(a, "p0b_ev", c.n("p0b_ev"))
        c.dma(a, "p0b_xpst", xp_loc[0:128, :], pb(2, 1024))
        c.dma(a, "p0b_xpst", xp_loc[128:160, :], pb(3, 1024)[0:32, :])
    blk.scalar(p0b_scalar2)

    emit_dbg("xT_dram")
    emit_dbg("xn1_loc")
    emit_dbg("xp_loc")

    barrier()

    GRPS = [[0, 1, 2, 3], [4, 5, 6, 7]]

    def e01(g):
        c.inc("ccxp", g.collective_compute("AllGather", ALU.bypass,
                                           replica_groups=GRPS,
                                           ins=[xp_loc.opt()], outs=[xp_ag.opt()]))
        c.inc("ccxn", g.collective_compute("AllGather", ALU.bypass,
                                           replica_groups=GRPS,
                                           ins=[xn1_loc.opt()], outs=[xn1_ag.opt()]))
        c.wait(g, "ccxp", c.n("ccxp"))
        for j in range(4):
            lo, hi = _chunks_of_core(j)
            for half, ch in ((0, lo), (1, hi)):
                src = xp_ag[j * 160 + 128: j * 160 + 160, half * 512:(half + 1) * 512]
                c.dma(g, "qgbc", bc_mine[:, ch * 512:(ch + 1) * 512], src)
        c.wait(g, "ccxn", c.n("ccxn"))
        pid = g.partition_id()
        q = pid % 4
        c.dma(g, "qg", xn1_mine.rearrange("(r d) t -> r d t", d=1024),
              xn1_ag.rearrange("(r d) t -> r d t", d=D)[:, bass.ts(q, 1024), :])
    blk.gpsimd(e01)
    QGBC_STAGED = c.n("qgbc")
    QG_XN1_STAGED = c.n("qg")
    emit_dbg("xp_ag")

    def p1_prolog_sync(sy):
        for i in range(8):
            c.dma(sy, "p1_pl", PS[:, i * 16:(i + 1) * 16], alog_mine[i * 128:(i + 1) * 128, :])
            c.dma(sy, "p1_pl", PS[:, 128 + i:129 + i], dtb_mine[i * 128:(i + 1) * 128, :])
            c.dma(sy, "p1_pl", PS[:, 136 + i:137 + i], dvec_mine[i * 128:(i + 1) * 128, :])
        c.dma(sy, "p1_pl", PS[:, 144:1168], dtw_mine)
    blk.sync(p1_prolog_sync)

    def p1_prolog_a(a):
        c.wait(a, "p1_pl", c.n("p1_pl"))
        inst = a.activation(out=PS[:, 0:128], in_=PS[:, 0:128], func=AF.Exp)
        c.inc("p1_pa", inst)
        inst = a.activation(out=WS[:, 1536:2560], in_=PS[:, 144:1168], func=AF.Identity)
        c.inc("p1_pa", inst)
    blk.scalar(p1_prolog_a)

    def p1_prolog_v(v):
        c.wait(v, "p1_pa", c.n("p1_pa"))
        inst = v.tensor_scalar_mul(PS[:, 0:128], PS[:, 0:128], -1.0)
        c.inc("p1_pv", inst)
    blk.vector(p1_prolog_v)

    TILES = list(range(8))

    def p1_sync(sy):
        c.wait(sy, "ccxp", c.n("ccxp"))
        for k, i in enumerate(TILES):
            for ch in range(8):
                j = ch if ch < 4 else 7 - ch
                half = 0 if ch < 4 else 1
                if k >= 1 or ch >= 2:
                    gchunk = k * 8 + ch
                    c.wait(sy, "p1_mm", gchunk - 1)
                src = xp_ag[j * 160:j * 160 + 128,
                            half * 512:(half + 1) * 512]
                c.dmap(sy, "p1_di", k * 8 + ch, 2,
                       WS[:, 256 + (ch % 2) * 512:256 + (ch % 2) * 512 + 512], src)
            if k == 0:
                c.wait(sy, "qg", QG_XN1_STAGED)
            if k >= 2:
                c.wait(sy, "p1_v", k - 1)
                c.wait(sy, "p1_yf", (k - 1) * 8)
            for j in range(4):
                lo, hi = _chunks_of_core(j)
                for half, ch in ((0, lo), (1, hi)):
                    src = xn1_mine[j * 1024 + i * 128:j * 1024 + (i + 1) * 128,
                                   half * 512:(half + 1) * 512]
                    c.dmap(sy, "p1_xn", k, 2, pb(k % 2)[:, ch * 512:(ch + 1) * 512], src)
    blk.sync(p1_sync)

    def p1_g(g):
        c.wait(g, "qgbc", QGBC_STAGED)
        for k, i in enumerate(TILES):
            for n in range(16):
                gn = k * 16 + n
                if gn >= 2:
                    c.wait(g, "p1_u", gn - 1)
                    c.wait(g, "p1_hc", gn - 1)
                c.dmap(g, "p1_bcB", gn, 2, pb(2 + n % 2),
                       bc_mine[n:n + 1, :].broadcast_to([128, T]))
                c.dmap(g, "p1_bcC", gn, 2, pb(4 + n % 2),
                       bc_mine[16 + n:17 + n, :].broadcast_to([128, T]))
    blk.gpsimd(p1_g)

    def p1_tensor(t):
        for k, i in enumerate(TILES):
            if k == 0:
                c.wait(t, "p1_pa", c.n("p1_pa"))
            off = 1536 + i * 128
            for ch in range(8):
                gchunk = k * 8 + ch
                c.wdmap(t, "p1_di", gchunk, 2)
                if k >= 1:
                    c.wait(t, "p1_yf", (k - 1) * 8 + ch + 1)
                inst = t.matmul(PSUM[ch][:, :], WS[:, off:off + 128],
                                WS[:, 256 + (ch % 2) * 512:256 + (ch % 2) * 512 + 512],
                                start=True, stop=True)
                c.inc("p1_mm", inst)
            for n in range(16):
                gn = k * 16 + n
                c.wait(t, "p1_hc", gn + 1)
                if n == 0:
                    c.wait(t, "p1_act3", (k * 8 + 8) * 3)
                    c.wait(t, "p1_spv", (k * 8 + 8) * 2)
                hc = PB2[:, (n % 2) * 4096:(n % 2) * 4096 + 4096]
                for ch in range(8):
                    inst = t.matmul(PSUM[ch][:, :], identb_sb[:],
                                    hc[:, ch * 512:(ch + 1) * 512],
                                    start=(n == 0), stop=(n == 15))
                    c.inc("p1_ymm", inst)
    blk.tensor(p1_tensor)

    def p1_a(a):
        c.wait(a, "p1_pl", c.n("p1_pl"))
        c.wait(a, "p1_pv", c.n("p1_pv"))
        for k, i in enumerate(TILES):
            if k >= 1:
                c.wait(a, "p1_spv", k * 16)
            for ch in range(8):
                gchunk = k * 8 + ch
                c.wait(a, "p1_mm", gchunk + 1)
                sl = pf(5)[:, ch * 512:(ch + 1) * 512]
                inst = a.activation(out=sl, in_=PSUM[ch][:, :], func=AF.Abs,
                                    bias=PS[:, 128 + i:129 + i])
                c.ser(a, "p1_act3", inst)
                inst = a.activation(out=sl, in_=sl, func=AF.Exp, scale=CONSTS[:, 4:5])
                c.ser(a, "p1_act3", inst)
                inst = a.activation(out=sl, in_=sl, func=AF.Ln, bias=CONSTS[:, 1:2])
                c.inc("p1_act3", inst)
            for n in range(16):
                gn = k * 16 + n
                if gn >= 2:
                    c.wait(a, "p1_scan", gn - 1)
                c.wait(a, "p1_spv", (k * 8 + 8) * 2)
                inst = a.activation(out=pf(1 + n % 2), in_=pf(0), func=AF.Exp,
                                    scale=PS[:, i * 16 + n:i * 16 + n + 1])
                c.inc("p1_dA", inst)
            c.wait(a, "p1_yf", (k + 1) * 8)
            c.dma(a, "p1_yst", y_loc[i * 128:(i + 1) * 128, :], pb(9))
    blk.scalar(p1_a)

    def p1_v(v):
        c.wait(v, "p1_pl", c.n("p1_pl"))
        for k, i in enumerate(TILES):
            for ch in range(8):
                gchunk = k * 8 + ch
                c.wait(v, "p1_mm", gchunk + 1)
                c.wait(v, "p1_act3", gchunk * 3 + 3)
                sl_dt = pf(0)[:, ch * 512:(ch + 1) * 512]
                sl_l = pf(5)[:, ch * 512:(ch + 1) * 512]
                inst = v.tensor_scalar(out=sl_dt, in0=PSUM[ch][:, :],
                                       scalar1=PS[:, 128 + i:129 + i], scalar2=0.0,
                                       op0=ALU.add, op1=ALU.max)
                c.ser(v, "p1_spv", inst)
                inst = v.tensor_add(sl_dt, sl_dt, sl_l)
                c.inc("p1_spv", inst)
            c.wait(v, "p1_spv", c.n("p1_spv"))
            c.wdmap(v, "p1_xn", k, 2, per_idx=8)
            inst = v.tensor_mul(pb(8), pf(0), pb(k % 2))
            c.ser(v, "p1_v", inst)
            hb = [pf(3), pf(4)]
            for n in range(16):
                gn = k * 16 + n
                c.wdmap(v, "p1_bcB", gn, 2)
                inst = v.tensor_mul(pb(6 + n % 2), pb(8), pb(2 + n % 2))
                c.ser(v, "p1_u", inst)
                c.wait(v, "p1_dA", gn + 1)
                if gn >= 2:
                    c.wait(v, "p1_hc", gn - 1)
                inst = v.tensor_tensor_scan(out=hb[n % 2], data0=pf(1 + n % 2),
                                            data1=pb(6 + n % 2), initial=0.0,
                                            op0=ALU.mult, op1=ALU.add)
                c.ser(v, "p1_scan", inst)
                c.wdmap(v, "p1_bcC", gn, 2)
                if gn >= 2:
                    c.wait(v, "p1_ymm", (gn - 2) * 8 + 8)
                inst = v.tensor_mul(PB2[:, (n % 2) * 4096:(n % 2) * 4096 + 4096],
                                    hb[n % 2], pb(4 + n % 2))
                c.inc("p1_hc", inst)
            if k >= 1:
                c.wait(v, "p1_yst", k * 16)
            for ch in range(8):
                c.wait(v, "p1_ymm", k * 128 + 15 * 8 + ch + 1)
                inst = v.scalar_tensor_tensor(
                    out=pb(9)[:, ch * 512:(ch + 1) * 512],
                    in0=pb(k % 2)[:, ch * 512:(ch + 1) * 512],
                    scalar=PS[:, 136 + i:137 + i], in1=PSUM[ch][:, :],
                    op0=ALU.mult, op1=ALU.add)
                c.inc("p1_yf", inst)
    blk.vector(p1_v)

    emit_dbg("y_loc")

    def e2(g):
        c.wait(g, "p1_yst", c.n("p1_yst"))
        c.inc("cc", g.collective_compute("AllGather", ALU.bypass,
                                         replica_groups=GRPS,
                                         ins=[y_loc.opt()], outs=[y_ag.opt()]))
        c.wait(g, "cc", c.n("cc"))
        pid = g.partition_id()
        ECH = [pid % 4, 7 - pid % 4]
        for half in range(2):
            c.dma(g, "qg", y_mine[:, half * 512:(half + 1) * 512],
                  y_ag[:, bass.ts(ECH[half], 512)])
        inst = g.engine_nop()
        c.inc("e2done", inst)
    blk.gpsimd(e2)
    QG_Y_STAGED = c.n("qg")

    def emit_resid_stats_scalar(tag, dg, stage, a, do_stats):
        if do_stats:
            off = (dg % 2) * 4096
            inst = a.activation(out=PB2[:, off:off + 1024], in_=stage, func=AF.Identity)
            c.inc(tag + "_cast", inst)
            inst = a.activation(out=PB2[:, off + 1024:off + 2048], in_=stage, func=AF.Square)
            c.inc(tag + "_cast", inst)
        c.dmap(a, tag + "_xst", dg, 2, xT_dram[dg * 128:(dg + 1) * 128, :], stage)

    def emit_resid_stats_tensor(tag):
        def r_t(t):
            for dg in range(32):
                c.wait(t, tag + "_cast", dg * 2 + 2)
                off = (dg % 2) * 4096
                emit_stats_mm(t, tag, dg, PB2[:, off:off + 1024],
                              PB2[:, off + 1024:off + 2048], last=31)
        blk.tensor(r_t)

    def p2a():
        tag = "p2a"

        def r_sync(sy):
            c.wait(sy, "p1_yst", c.n("p1_yst"))
            for dg in range(32):
                if dg >= 2:
                    c.wait(sy, tag + "_add", (dg - 2) * 2 + 2)
                c.dmap(sy, tag + "_ld", dg, 2, pf(dg % 2, 1024),
                       xT_dram[dg * 128:(dg + 1) * 128, :])
                if dg == 0:
                    c.wait(sy, "qg", QG_Y_STAGED)
                c.dmap(sy, tag + "_ld", dg, 2, pb(dg % 2, 1024),
                       y_mine[dg * 128:(dg + 1) * 128, :])
        blk.sync(r_sync)

        def r_v(v):
            for dg in range(32):
                c.wdmap(v, tag + "_ld", dg, 2, per_idx=2)
                if dg >= 2:
                    c.wait(v, tag + "_cast", (dg - 2) * 2 + 2)
                    c.wdmap(v, tag + "_xst", dg - 2, 2)
                for ntile in range(2):
                    sl = slice(ntile * 512, (ntile + 1) * 512)
                    inst = v.tensor_add(pf(2 + dg % 2, 1024)[:, sl], pf(dg % 2, 1024)[:, sl],
                                        pb(dg % 2, 1024)[:, sl])
                    c.inc(tag + "_add", inst)
        blk.vector(r_v)

        def r_a(a):
            c.wait(a, "p1_yst", c.n("p1_yst"))
            for dg in range(32):
                c.wait(a, tag + "_add", dg * 2 + 2)
                c.wait(a, tag + "_mm", max(0, (dg - 1) * 4))
                emit_resid_stats_scalar(tag, dg, pf(2 + dg % 2, 1024), a, True)
        blk.scalar(r_a)

        emit_resid_stats_tensor(tag)
    p2a()
    emit_ln_finalize("ln2", "p2a")
    barrier()
    emit_dbg("x1")

    def emit_ffn(tag, stats_next):
        def a_sync(sy):
            for dg in range(32):
                if dg >= 2:
                    c.wait(sy, tag + "_xn", (dg - 2) * 2 + 2)
                c.dmap(sy, tag + "_ld", dg, 2, pf(dg % 2, 1024),
                       xT_dram[dg * 128:(dg + 1) * 128, :])
        blk.sync(a_sync)

        def a_g(g):
            for dg in range(32):
                if dg >= 2:
                    c.wait(g, tag + "_amm", (dg - 1) * 4)
                off = (dg % 2) * 256
                c.dmap(g, tag + "_w", dg, 2, WS[:, off:off + 64],
                       ffnAT[dg * 128:(dg + 1) * 128, :])
                c.dmap(g, tag + "_w", dg, 2, WS[:, off + 64:off + 68],
                       routT[dg * 128:(dg + 1) * 128, :])
        blk.gpsimd(a_g)

        def a_v(v):
            for dg in range(32):
                c.wdmap(v, tag + "_ld", dg, 2)
                if dg >= 2:
                    c.wait(v, tag + "_amm", (dg - 1) * 4)
                emit_xn_tile(v, tag + "_xn", pf(dg % 2, 1024), pf(2, 1024), pb(dg % 2, 1024))
        blk.vector(a_v)

        def a_t(t):
            for dg in range(32):
                c.wdmap(t, tag + "_w", dg, 2, per_idx=2)
                c.wait(t, tag + "_xn", dg * 2 + 2)
                off = (dg % 2) * 256
                xn = pb(dg % 2, 1024)
                for ntile in range(2):
                    inst = t.matmul(PSUM[ntile][0:ER, :], WS[:, off:off + 64],
                                    xn[:, ntile * 512:(ntile + 1) * 512],
                                    start=(dg == 0), stop=(dg == 31))
                    c.inc(tag + "_amm", inst)
                for ntile in range(2):
                    inst = t.matmul(PSUM[2 + ntile][0:NE, :], WS[:, off + 64:off + 68],
                                    xn[:, ntile * 512:(ntile + 1) * 512],
                                    start=(dg == 0), stop=(dg == 31))
                    c.inc(tag + "_amm", inst)
        blk.tensor(a_t)

        def b_v1(v):
            c.wait(v, tag + "_amm", c.n(tag + "_amm"))
            v.tensor_copy(PS[0:4, 0:512], PSUM[2][0:4, :])
            inst = v.tensor_copy(PS[0:4, 512:1024], PSUM[3][0:4, :])
            c.inc(tag + "_lev", inst)
        blk.vector(b_v1)

        def b_t1(t):
            c.wait(t, tag + "_lev", c.n(tag + "_lev"))
            for tt in range(8):
                inst = t.transpose(PSUM[2][:, tt * 4:(tt + 1) * 4],
                                   PS[0:4, tt * 128:(tt + 1) * 128], identf_sb[0:4, 0:4])
                c.inc(tag + "_ltr", inst)
        blk.tensor(b_t1)

        def b_v2(v):
            c.wait(v, tag + "_ltr", c.n(tag + "_ltr"))
            lg = PS[:, 0:32].rearrange("p (a e) -> p a e", e=4)
            inst = v.tensor_copy(PS[:, 0:32], PSUM[2][:, 0:32])
            c.ser(v, tag + "_gc", inst)
            cnt = PS[:, 64:96].rearrange("p (a e) -> p a e", e=4)
            tmp = PS[:, 32:64].rearrange("p (a e) -> p a e", e=4)
            inst = v.memset(PS[:, 64:96], 0.0)
            c.ser(v, tag + "_gc", inst)
            for e2 in range(4):
                for e1 in range(4):
                    if e1 == e2:
                        continue
                    inst = v.tensor_tensor(out=tmp[:, :, e2], in0=lg[:, :, e1], in1=lg[:, :, e2],
                                           op=ALU.is_gt)
                    c.ser(v, tag + "_gc", inst)
                    inst = v.tensor_add(cnt[:, :, e2], cnt[:, :, e2], tmp[:, :, e2])
                    c.ser(v, tag + "_gc", inst)
            inst = v.tensor_single_scalar(PS[:, 64:96], PS[:, 64:96], 1.5, ALU.is_le)
            c.ser(v, tag + "_gc", inst)
            inst = v.reduce_max(PS[:, 96:104], lg, axis=AXX)
            c.ser(v, tag + "_gc", inst)
            inst = v.engine_nop()
            c.inc(tag + "_g1", inst)
        blk.vector(b_v2)


        def b_v3(v):
            lg = PS[:, 0:32].rearrange("p (a e) -> p a e", e=4)
            lm = PS[:, 32:64].rearrange("p (a e) -> p a e", e=4)
            for e in range(4):
                inst = v.tensor_tensor(out=lm[:, :, e], in0=lg[:, :, e], in1=PS[:, 96:104],
                                       op=ALU.subtract)
                c.ser(v, tag + "_gc", inst)
            inst = v.engine_nop()
            c.inc(tag + "_g2", inst)
        blk.vector(b_v3)

        def b_a2(a):
            c.wait(a, tag + "_g2", c.n(tag + "_g2"))
            inst = a.activation(out=PS[:, 32:64], in_=PS[:, 32:64], func=AF.Exp)
            c.inc(tag + "_g3", inst)
        blk.scalar(b_a2)

        def b_v4(v):
            c.wait(v, tag + "_g3", c.n(tag + "_g3"))
            inst = v.tensor_mul(PS[:, 32:64], PS[:, 32:64], PS[:, 64:96])
            c.ser(v, tag + "_gc", inst)
            ek = PS[:, 32:64].rearrange("p (a e) -> p a e", e=4)
            inst = v.reduce_sum(PS[:, 96:104], ek, axis=AXX)
            c.ser(v, tag + "_gc", inst)
            inst = v.reciprocal(PS[:, 104:112], PS[:, 96:104])
            c.ser(v, tag + "_gc", inst)
            for e in range(4):
                inst = v.tensor_tensor(out=ek[:, :, e], in0=ek[:, :, e], in1=PS[:, 104:112],
                                       op=ALU.mult)
                c.ser(v, tag + "_gc", inst)
            inst = v.tensor_single_scalar(PS[:, 32:64], PS[:, 32:64], LORA_SCALE, ALU.mult)
            c.inc(tag + "_g4", inst)
        blk.vector(b_v4)

        def b_t2(t):
            c.wait(t, tag + "_g4", c.n(tag + "_g4"))
            for tt in range(8):
                inst = t.transpose(PSUM[2 + tt // 4][0:4, (tt % 4) * 128:(tt % 4) * 128 + 128],
                                   PS[:, 32 + tt * 4:36 + tt * 4], identf_sb[:])
                c.inc(tag + "_gtr", inst)
        blk.tensor(b_t2)

        def b_v5(v):
            c.wait(v, tag + "_gtr", c.n(tag + "_gtr"))
            v.tensor_copy(pb(4, 1024)[0:4, 0:512], PSUM[2][0:4, :])
            inst = v.tensor_copy(pb(4, 1024)[0:4, 512:1024], PSUM[3][0:4, :])
            c.inc(tag + "_gev", inst)
        blk.vector(b_v5)

        def b_t3(t):
            c.wait(t, tag + "_gev", c.n(tag + "_gev"))
            for ntile in range(2):
                inst = t.matmul(PSUM[2 + ntile][0:ER, :], sel16_sb[:],
                                pb(4, 1024)[0:4, ntile * 512:(ntile + 1) * 512],
                                start=True, stop=True)
                c.inc(tag + "_grep", inst)
        blk.tensor(b_t3)

        def b_v6(v):
            c.wait(v, tag + "_grep", c.n(tag + "_grep"))
            v.tensor_copy(pb(2, 1024)[0:ER, 0:512], PSUM[2][0:ER, :])
            inst = v.tensor_copy(pb(2, 1024)[0:ER, 512:1024], PSUM[3][0:ER, :])
            c.ser(v, tag + "_gc", inst)
            for ntile in range(2):
                sl = slice(ntile * 512, (ntile + 1) * 512)
                inst = v.tensor_mul(pb(3, 1024)[0:ER, sl], PSUM[ntile][0:ER, :],
                                    pb(2, 1024)[0:ER, sl])
            c.inc(tag + "_midg", inst)
        blk.vector(b_v6)

        def c_sync(sy):
            c.wait(sy, tag + "_xn", 64)
            for dg in range(32):
                if dg >= 2:
                    c.wait(sy, tag + "_upd", (dg - 2) * 2 + 2)
                c.dmap(sy, tag + "_xld", dg, 2, pf(dg % 2, 1024),
                       xT_dram[dg * 128:(dg + 1) * 128, :])
        blk.sync(c_sync)

        def c_g(g):
            for dg in range(32):
                if dg >= 2:
                    c.wait(g, tag + "_dmm", (dg - 1) * 2)
                off = 512 + (dg % 2) * 128
                c.dmap(g, tag + "_bw", dg, 2, WS[0:ER, off:off + 128],
                       ffnB[:, dg * 128:(dg + 1) * 128])
        blk.gpsimd(c_g)

        def c_t(t):
            c.wait(t, tag + "_midg", c.n(tag + "_midg"))
            for dg in range(32):
                c.wdmap(t, tag + "_bw", dg, 2)
                if dg >= 2:
                    c.wait(t, tag + "_upd", (dg - 2) * 2 + 2)
                off = 512 + (dg % 2) * 128
                bp = (dg % 2) * 2
                for ntile in range(2):
                    inst = t.matmul(PSUM[bp + ntile][:, :], WS[0:ER, off:off + 128],
                                    pb(3, 1024)[0:ER, ntile * 512:(ntile + 1) * 512],
                                    start=True, stop=True)
                    c.inc(tag + "_dmm", inst)
                if stats_next and dg >= 1:
                    sdg = dg - 1
                    c.wait(t, tag + "_cast", (sdg + 1) * 2)
                    off2 = (sdg % 2) * 4096
                    emit_stats_mm(t, tag, sdg, PB2[:, off2:off2 + 1024],
                                  PB2[:, off2 + 1024:off2 + 2048], last=31)
            if stats_next:
                c.wait(t, tag + "_cast", 64)
                off2 = (31 % 2) * 4096
                emit_stats_mm(t, tag, 31, PB2[:, off2:off2 + 1024],
                              PB2[:, off2 + 1024:off2 + 2048], last=31)
        blk.tensor(c_t)

        def c_v(v):
            for dg in range(32):
                c.wdmap(v, tag + "_xld", dg, 2)
                c.wait(v, tag + "_dmm", dg * 2 + 2)
                if dg >= 2:
                    if stats_next:
                        c.wait(v, tag + "_cast", (dg - 2) * 2 + 2)
                    c.wdmap(v, tag + "_xst", dg - 2, 2)
                bp = (dg % 2) * 2
                for ntile in range(2):
                    sl = slice(ntile * 512, (ntile + 1) * 512)
                    inst = v.tensor_add(pf(2 + dg % 2, 1024)[:, sl], pf(dg % 2, 1024)[:, sl],
                                        PSUM[bp + ntile][:, :])
                    c.inc(tag + "_upd", inst)
        blk.vector(c_v)

        def c_a(a):
            for dg in range(32):
                c.wait(a, tag + "_upd", dg * 2 + 2)
                if stats_next:
                    c.wait(a, tag + "_mm", max(0, (dg - 1) * 4))
                emit_resid_stats_scalar(tag, dg, pf(2 + dg % 2, 1024), a, stats_next)
        blk.scalar(c_a)

    emit_ffn("f1", stats_next=True)
    emit_ln_finalize("ln3", "f1")
    emit_dbg("x2")

    barrier()

    def p3_pre_g(g):
        c.dma(g, "p3_cs", cos_sb, cosd)
        c.dma(g, "p3_cs", sin_sb, sind)
    blk.gpsimd(p3_pre_g)

    def p3_sync(sy):
        for dg in range(32):
            if dg >= 2:
                c.wait(sy, "p3_xn", (dg - 2) * 2 + 2)
            c.dmap(sy, "p3_ld", dg, 2, pf(dg % 2, 1024), xT_dram[dg * 128:(dg + 1) * 128, :])
    blk.sync(p3_sync)

    def p3_v1(v):
        for dg in range(32):
            c.wdmap(v, "p3_ld", dg, 2)
            emit_xn_tile(v, "p3_xn", pf(dg % 2, 1024), pf(2, 1024), PB[:, dg * 1024:(dg + 1) * 1024])
    blk.vector(p3_v1)

    NT_QKV = 48

    def p3_g(sy2):
        sy2
        c.wait(sy2, "p3_xn", 64)
        for ot in range(NT_QKV):
            if ot >= 2:
                c.wait(sy2, "p3_w", ot - 1)
            if ot < 32:
                w, col = wqT, ot * 128
            elif ot < 40:
                w, col = wkT, (ot - 32) * 128
            else:
                w, col = wvT, (ot - 40) * 128
            src = w[:, col:col + 128].rearrange("(a p) w -> p a w", p=128)
            c.dmap(sy2, "p3_wf", ot, 2, pf(ot % 2), src)
    blk.sync(p3_g)

    def p3_t(t):
        c.wait(t, "p3_xn", c.n("p3_xn"))
        for ot in range(NT_QKV):
            c.wait(t, "p3_w", ot + 1)
            bp = (ot % 2) * 2
            if ot >= 2:
                c.wait(t, "p3_ev", (ot - 1) * 2)
            band = pb(8 + ot % 2)
            for kc in range(32):
                for ntile in range(2):
                    inst = t.matmul(PSUM[bp + ntile][:, :], band[:, kc * 128:(kc + 1) * 128],
                                    PB[:, kc * 1024 + ntile * 512: kc * 1024 + ntile * 512 + 512],
                                    start=(kc == 0), stop=(kc == 31))
                    c.inc("p3_mm", inst)
    blk.tensor(p3_t)

    def p3_wc(v, ot):
        c.wdmap(v, "p3_wf", ot, 2)
        if ot >= 2:
            c.wait(v, "p3_mm", (ot - 1) * 64)
        inst = v.tensor_copy(pb(8 + ot % 2), pf(ot % 2))
        c.inc("p3_w", inst)

    def p3_v2(v):
        c.wait(v, "p3_cs", 32)
        p3_wc(v, 0)
        p3_wc(v, 1)
        for ot in range(NT_QKV):
            if ot + 2 < NT_QKV:
                p3_wc(v, ot + 2)
            c.wait(v, "p3_mm", (ot + 1) * 64)
            if ot >= 2:
                c.wdmap(v, "p3_kst", ot - 2, 2)
            bp = (ot % 2) * 2
            off = (ot % 2) * 2048
            outsl = PB2[:, off + 1024:off + 2048]
            if ot < 40:
                tmp = PB2[:, off:off + 1024]
                for ntile in range(2):
                    sl = slice(ntile * 512, (ntile + 1) * 512)
                    ps = PSUM[bp + ntile]
                    v.tensor_scalar_mul(tmp[0:64, sl], ps[64:128, :], -1.0)
                    inst = v.tensor_copy(tmp[64:128, sl], ps[0:64, :])
                    c.ser(v, "p3_rp", inst)
                    v.tensor_mul(pf(4, 1024)[:, sl], ps[:, :], cos_sb[:, sl])
                    inst = v.tensor_mul(pf(5, 1024)[:, sl], tmp[:, sl], sin_sb[:, sl])
                    c.ser(v, "p3_rp", inst)
                    inst = v.tensor_add(outsl[:, sl], pf(4, 1024)[:, sl], pf(5, 1024)[:, sl])
                    c.inc("p3_ev", inst)
            else:
                for ntile in range(2):
                    sl = slice(ntile * 512, (ntile + 1) * 512)
                    inst = v.tensor_copy(outsl[:, sl], PSUM[bp + ntile][:, :])
                    c.inc("p3_ev", inst)
    blk.vector(p3_v2)

    def p3_a(a):
        for ot in range(NT_QKV):
            c.wait(a, "p3_ev", (ot + 1) * 2)
            off = (ot % 2) * 2048
            outsl = PB2[:, off + 1024:off + 2048]
            if ot < 32:
                dst = kvq_loc[2048 + ot * 128: 2048 + (ot + 1) * 128, :]
            elif ot < 40:
                dst = kvq_loc[(ot - 32) * 128:(ot - 32) * 128 + 128, :]
            else:
                dst = xn1_loc[(ot - 40) * 128:(ot - 40) * 128 + 128, :]
            c.dmap(a, "p3_kst", ot, 2, dst, outsl)
    blk.scalar(p3_a)

    def p3_vt_sync(sy):
        c.wait(sy, "p3_kst0", c.n("p3_kst0"))
        c.wait(sy, "p3_kst1", c.n("p3_kst1"))
        for vt in range(8):
            if vt >= 2:
                c.wait(sy, "p3_vtr", (vt - 1) * 8)
            c.dmap(sy, "p3_vld", vt, 2, pb(8 + vt % 2, 1024),
                   xn1_loc[vt * 128:(vt + 1) * 128, :])
    blk.sync(p3_vt_sync)

    def p3_vt_t(t):
        for vt in range(8):
            c.wdmap(t, "p3_vld", vt, 2)
            for tt in range(8):
                g = vt * 8 + tt
                if g >= 4:
                    c.wait(t, "p3_vev", g - 3)
                inst = t.transpose(PSUM[g % 4][:, 0:128].bitcast(BF16)[:, 0:128],
                                   pb(8 + vt % 2, 1024)[:, tt * 128:(tt + 1) * 128],
                                   identb_sb[:])
                c.inc("p3_vtr", inst)
    blk.tensor(p3_vt_t)

    def p3_vt_v(v):
        for vt in range(8):
            for tt in range(8):
                g = vt * 8 + tt
                c.wait(v, "p3_vtr", g + 1)
                if g >= 2:
                    c.wdmap(v, "p3_vst", g - 2, 2)
                inst = v.tensor_copy(PB2[:, (g % 2) * 1024:(g % 2) * 1024 + 128],
                                     PSUM[g % 4][:, 0:128].bitcast(BF16)[:, 0:128])
                c.inc("p3_vev", inst)
    blk.vector(p3_vt_v)

    def p3_vt_a(a):
        for vt in range(8):
            for tt in range(8):
                g = vt * 8 + tt
                c.wait(a, "p3_vev", g + 1)
                dst = kvq_loc[1024 + vt * 128:1024 + (vt + 1) * 128, tt * 128:(tt + 1) * 128]
                c.dmap(a, "p3_vst", g, 2, dst, PB2[:, (g % 2) * 1024:(g % 2) * 1024 + 128])
    blk.scalar(p3_vt_a)

    emit_dbg("kvq_loc")

    def e3(g):
        c.wait(g, "p3_vst0", c.n("p3_vst0"))
        c.wait(g, "p3_vst1", c.n("p3_vst1"))
        c.wait(g, "p3_kst0", c.n("p3_kst0"))
        c.wait(g, "p3_kst1", c.n("p3_kst1"))
        c.inc("cc", g.collective_compute("AllGather", ALU.bypass,
                                         replica_groups=[[0, 1, 2, 3], [4, 5, 6, 7]],
                                         ins=[kvq_loc.opt()], outs=[kvq_ag.opt()]))
        c.wait(g, "cc", c.n("cc"))
        pid = g.partition_id()
        EJ = pid % 4
        kvqr = kvq_ag.rearrange("(r x) t -> r x t", x=KVQ_ROWS)
        c.dma(g, "qg", q_mine.rearrange("(r d) t -> r d t", d=1024),
              kvqr[:, 2048:6144, :][:, bass.ts(EJ, 1024), :])
        c.dma(g, "qg", k_mine.rearrange("(r d) t -> r d t", d=256),
              kvqr[:, 0:1024, :][:, bass.ts(EJ, 256), :])
        c.dma(g, "qg", v_mine.rearrange("(r d) t -> r d t", d=256),
              kvqr[:, 1024:2048, :][:, bass.ts(EJ, 256), :])
        inst = g.engine_nop()
        c.inc("e3done", inst)
    blk.gpsimd(e3)
    barrier()
    emit_dbg("q_mine")

    def nb_of(chunk):
        return 4 * chunk + 4

    def p4_pre_g(g):
        c.dma(g, "p4_mk", masks_sb, masksd)
    blk.gpsimd(p4_pre_g)

    def p4_sync(sy):
        for hh in range(8):
            if hh >= 2:
                c.wait(sy, "p4_done", (hh - 1) * 8)
            for rb in range(4):
                lo, hi = _chunks_of_core(rb)
                for half, ch in ((0, lo), (1, hi)):
                    src = q_mine[rb * 1024 + hh * 128:rb * 1024 + (hh + 1) * 128,
                                 half * 512:(half + 1) * 512]
                    c.dmap(sy, "p4_q", hh, 2, pb(hh % 2)[:, ch * 512:(ch + 1) * 512], src)
            for rb in range(4):
                lo, hi = _chunks_of_core(rb)
                for half, ch in ((0, lo), (1, hi)):
                    kh = hh // 4
                    src = k_mine[rb * 256 + kh * 128:rb * 256 + (kh + 1) * 128,
                                 half * 512:(half + 1) * 512]
                    c.dmap(sy, "p4_k", hh, 2, pb(2 + hh % 2)[:, ch * 512:(ch + 1) * 512], src)
            for rb in range(4):
                lo, hi = _chunks_of_core(rb)
                for half, ch in ((0, lo), (1, hi)):
                    kh = hh // 4
                    src = v_mine[rb * 256 + kh * 128:rb * 256 + (kh + 1) * 128,
                                 half * 512:(half + 1) * 512]
                    dst = pb(4 + hh % 2)[:, ch * 512:(ch + 1) * 512]
                    c.dmap(sy, "p4_v", hh, 2, dst, src)
    blk.sync(p4_sync)

    def p4_t(t):
        for hh in range(8):
            c.wdmap(t, "p4_q", hh, 2, per_idx=8)
            c.wdmap(t, "p4_k", hh, 2, per_idx=8)
            c.wdmap(t, "p4_v", hh, 2, per_idx=8)
            blocks = []
            for chunk in range(8):
                for tb in range(nb_of(chunk)):
                    blocks.append((chunk, tb))
            nblk = len(blocks)

            SB = [0, 1, 7]

            def s_mm(j):
                chunk, tb = blocks[j]
                gt = c.n("p4_smm")
                if gt >= 3:
                    c.wait(t, "p4_exp", gt - 2)
                if tb == 0 and hh * 8 + chunk >= 2:
                    c.wait(t, "p4_out", hh * 8 + chunk - 1)
                inst = t.matmul(PSUM[SB[gt % 3]][:, :], pb(2 + hh % 2)[:, tb * 128:(tb + 1) * 128],
                                pb(hh % 2)[:, chunk * 512:(chunk + 1) * 512],
                                start=True, stop=True)
                c.inc("p4_smm", inst)

            s_mm(0)
            if nblk > 1:
                s_mm(1)
            for j in range(nblk):
                if j + 2 < nblk:
                    s_mm(j + 2)
                chunk, tb = blocks[j]
                gt = hh * 144 + sum(nb_of(c2) for c2 in range(chunk)) + tb
                nb = nb_of(chunk)
                gc = hh * 8 + chunk
                c.wait(t, "p4_exp", gt + 1)
                pt = PB2[:, (gt % 3) * 512:(gt % 3) * 512 + 512]
                inst = t.matmul(PSUM[2 + chunk % 2][0:1, :], ones_bf[:, 0:1], pt,
                                start=(tb == 0), stop=(tb == nb - 1))
                c.inc("p4_lmm", inst)
                inst = t.matmul(PSUM[4 + chunk % 2][:, :],
                                pb(4 + hh % 2)[:, tb * 128:(tb + 1) * 128], pt,
                                start=(tb == 0), stop=(tb == nb - 1))
                c.inc("p4_pmm", inst)
                if tb == nb - 1:
                    c.wait(t, "p4_rcp", gc + 1)
                    c.wait(t, "p4_rev", gc)
                    inst = t.matmul(PSUM[6][:, :], ones_f[0:1, :],
                                    PS[0:1, (gc % 2) * 512:(gc % 2) * 512 + 512],
                                    start=True, stop=True)
                    c.inc("p4_rbc", inst)
    blk.tensor(p4_t)

    def p4_v_mask(v):
        c.wait(v, "p4_mk", 16)
        for hh in range(8):
            for chunk in range(8):
                nb = nb_of(chunk)
                gc = hh * 8 + chunk
                for tb in range(nb):
                    gt = hh * 144 + sum(nb_of(ch2) for ch2 in range(chunk)) + tb
                    if tb >= 4 * chunk:
                        toff = tb - 4 * chunk
                        c.wait(v, "p4_smm", gt + 1)
                        sbk = (0, 1, 7)[gt % 3]
                        inst = v.tensor_add(PSUM[sbk][:, :], PSUM[sbk][:, :],
                                            masks_sb[:, 384 - 128 * toff:896 - 128 * toff])
                        c.inc("p4_msk", inst)
                c.wait(v, "p4_lmm", c_lmm_at[gc])
                inst = v.reciprocal(PS[0:1, (gc % 2) * 512:(gc % 2) * 512 + 512],
                                    PSUM[2 + chunk % 2][0:1, :])
                c.inc("p4_rcp", inst)
                c.wait(v, "p4_rbc", gc + 1)
                inst = v.tensor_copy(PS[:, 1024 + (gc % 2) * 512:1024 + (gc % 2) * 512 + 512],
                                     PSUM[6][:, :])
                c.ser(v, "p4_rev", inst)
                c.wait(v, "p4_pmm", c_pmm_at[gc])
                if gc >= 2:
                    c.wdmap(v, "p4_ast", gc - 2, 2)
                inst = v.tensor_mul(PB2[:, 1536 + (gc % 2) * 512:1536 + (gc % 2) * 512 + 512],
                                    PSUM[4 + chunk % 2][:, :],
                                    PS[:, 1024 + (gc % 2) * 512:1024 + (gc % 2) * 512 + 512])
                c.inc("p4_out", inst)
    c_lmm_at = {}
    c_pmm_at = {}
    acc = 0
    for hh in range(8):
        for chunk in range(8):
            acc += nb_of(chunk)
            c_lmm_at[hh * 8 + chunk] = acc
            c_pmm_at[hh * 8 + chunk] = acc
    blk.vector(p4_v_mask)

    def p4_a(a):
        for hh in range(8):
            for chunk in range(8):
                nb = nb_of(chunk)
                gc = hh * 8 + chunk
                for tb in range(nb):
                    gt = hh * 144 + sum(nb_of(ch2) for ch2 in range(chunk)) + tb
                    c.wait(a, "p4_smm", gt + 1)
                    if tb >= 4 * chunk:
                        c.wait(a, "p4_msk", msk_count_at[gt])
                    pt = PB2[:, (gt % 3) * 512:(gt % 3) * 512 + 512]
                    if gt >= 3:
                        c.wait(a, "p4_lmm", gt - 2)
                        c.wait(a, "p4_pmm", gt - 2)
                    inst = a.activation(out=pt, in_=PSUM[(0, 1, 7)[gt % 3]][:, :], func=AF.Exp,
                                        scale=CONSTS[:, 5:6], bias=CONSTS[:, 2:3])
                    c.inc("p4_exp", inst)
                c.wait(a, "p4_out", gc + 1)
                dst = attn_loc[hh * 128:(hh + 1) * 128, chunk * 512:(chunk + 1) * 512]
                c._get("p4_ast" + str(gc % 2), kind=1)
                inst = a.dma_start(out=dst,
                                   in_=PB2[:, 1536 + (gc % 2) * 512:1536 + (gc % 2) * 512 + 512])
                c.inc("p4_ast" + str(gc % 2), inst, 16)
                inst = a.nop()
                c.inc("p4_done", inst)
    msk_count_at = {}
    mc = 0
    for hh in range(8):
        for chunk in range(8):
            nb = nb_of(chunk)
            for tb in range(nb):
                gt = hh * 144 + sum(nb_of(ch2) for ch2 in range(chunk)) + tb
                if tb >= 4 * chunk:
                    mc += 1
                    msk_count_at[gt] = mc
    blk.scalar(p4_a)

    emit_dbg("attn_loc")

    def e4(g):
        c.wait(g, "p4_ast0", c.n("p4_ast0"))
        c.wait(g, "p4_ast1", c.n("p4_ast1"))
        c.inc("cc", g.collective_compute("AllGather", ALU.bypass,
                                         replica_groups=[[0, 1, 2, 3], [4, 5, 6, 7]],
                                         ins=[attn_loc.opt()], outs=[attn_ag.opt()]))
        c.wait(g, "cc", c.n("cc"))
        inst = g.engine_nop()
        c.inc("e4done", inst)
    blk.gpsimd(e4)
    barrier()

    def p5_sync0(sy):
        pid = sy.partition_id()
        ECH = [pid % 4, 7 - pid % 4]
        for half in range(2):
            c.dma(sy, "p5_stg", attn_mine[:, half * 512:(half + 1) * 512],
                  attn_ag[:, bass.ts(ECH[half], 512)])
        c.wait(sy, "p5_stg", c.n("p5_stg"))
        for oc in range(32):
            c.dma(sy, "p5_ald", PB[:, oc * 1024:(oc + 1) * 1024],
                  attn_mine[oc * 128:(oc + 1) * 128, :])
    blk.sync(p5_sync0)

    def p5_sync1(sy):
        for ht in range(32):
            if ht >= 2:
                c.wait(sy, "p5_upd", (ht - 2) * 2 + 2)
                c.wait(sy, "p5_w", ht - 1)
            c.dmap(sy, "p5_xld", ht, 2, pf(ht % 2, 1024), xT_dram[ht * 128:(ht + 1) * 128, :])
            srcw = woT[:, ht * 128:(ht + 1) * 128].rearrange("(a p) w -> p a w", p=128)
            c.dmap(sy, "p5_wf", ht, 2, pf(4 + ht % 2), srcw)
    blk.sync(p5_sync1)

    def p5_t(t):
        c.wait(t, "p5_ald", c.n("p5_ald"))
        for ht in range(32):
            c.wait(t, "p5_w", ht + 1)
            bp = (ht % 2) * 2
            if ht >= 2:
                c.wait(t, "p5_upd", (ht - 2) * 2 + 2)
            band = pb(8 + ht % 2)
            for kc in range(32):
                for ntile in range(2):
                    inst = t.matmul(PSUM[bp + ntile][:, :], band[:, kc * 128:(kc + 1) * 128],
                                    PB[:, kc * 1024 + ntile * 512:kc * 1024 + ntile * 512 + 512],
                                    start=(kc == 0), stop=(kc == 31))
                    c.inc("p5_mm", inst)
            if ht >= 1:
                sdg = ht - 1
                c.wait(t, "p5a_cast", (sdg + 1) * 2)
                off2 = (sdg % 2) * 4096
                emit_stats_mm(t, "p5a", sdg, PB2[:, off2:off2 + 1024],
                              PB2[:, off2 + 1024:off2 + 2048], last=31)
        c.wait(t, "p5a_cast", 64)
        off2 = (31 % 2) * 4096
        emit_stats_mm(t, "p5a", 31, PB2[:, off2:off2 + 1024],
                      PB2[:, off2 + 1024:off2 + 2048], last=31)
    blk.tensor(p5_t)

    def p5_wc(v, ht):
        c.wdmap(v, "p5_wf", ht, 2)
        if ht >= 2:
            c.wait(v, "p5_mm", (ht - 1) * 64)
        inst = v.tensor_copy(pb(8 + ht % 2), pf(4 + ht % 2))
        c.inc("p5_w", inst)

    def p5_v(v):
        p5_wc(v, 0)
        p5_wc(v, 1)
        for ht in range(32):
            c.wdmap(v, "p5_xld", ht, 2)
            c.wait(v, "p5_mm", (ht + 1) * 64)
            if ht >= 2:
                c.wait(v, "p5a_cast", (ht - 2) * 2 + 2)
                c.wdmap(v, "p5a_xst", ht - 2, 2)
            bp = (ht % 2) * 2
            for ntile in range(2):
                sl = slice(ntile * 512, (ntile + 1) * 512)
                inst = v.tensor_add(pf(2 + ht % 2, 1024)[:, sl], pf(ht % 2, 1024)[:, sl],
                                    PSUM[bp + ntile][:, :])
                c.inc("p5_upd", inst)
            if ht + 2 < 32:
                p5_wc(v, ht + 2)
    blk.vector(p5_v)

    def p5_a(a):
        for ht in range(32):
            c.wait(a, "p5_upd", ht * 2 + 2)
            c.wait(a, "p5a_mm", max(0, (ht - 1) * 4))
            emit_resid_stats_scalar("p5a", ht, pf(2 + ht % 2, 1024), a, True)
    blk.scalar(p5_a)
    emit_ln_finalize("ln4", "p5a")
    barrier()
    emit_dbg("x3")

    emit_ffn("f2", stats_next=False)
    emit_dbg("x4")

    barrier()

    def p6_sync(sy):
        for tt in range(8):
            for dg in range(32):
                g = tt * 32 + dg
                if g >= 8:
                    c.wait(sy, "p6_tr", g - 7)
                src = xT_dram[dg * 128:(dg + 1) * 128, tt * 128:(tt + 1) * 128]
                c.dmap(sy, "p6_ld", g, 8, pf(0, 1024)[:, (g % 8) * 128:(g % 8) * 128 + 128], src)
            c.wait(sy, "p6_ev", (tt + 1) * 32)
            c.dma(sy, "p6_st", out_rows[tt * 128:(tt + 1) * 128, :], pf(2))
        c.wait(sy, "p6_st", c.n("p6_st"))
    blk.sync(p6_sync)

    def p6_t(t):
        for tt in range(8):
            for dg in range(32):
                g = tt * 32 + dg
                c.wdmap(t, "p6_ld", g, 8)
                if g >= 4:
                    c.wait(t, "p6_ev", g - 3)
                inst = t.transpose(PSUM[g % 4][:, 0:128],
                                   pf(0, 1024)[:, (g % 8) * 128:(g % 8) * 128 + 128],
                                   identf_sb[:])
                c.inc("p6_tr", inst)
    blk.tensor(p6_t)

    def p6_v(v):
        for tt in range(8):
            if tt >= 1:
                c.wait(v, "p6_st", tt * 16)
            for dg in range(32):
                g = tt * 32 + dg
                c.wait(v, "p6_tr", g + 1)
                inst = v.tensor_copy(pf(2)[:, dg * 128:(dg + 1) * 128], PSUM[g % 4][:, 0:128])
                c.inc("p6_ev", inst)
    blk.vector(p6_v)

    barrier()
    blk.__exit__(None, None, None)
    nc._kctx = c
    return nc, [nm for nm in dbg]


def _host_consts(cid):
    pos = _tok_positions(cid)
    cos, sin = _llama3_cos_sin(pos)
    ident = np.eye(128, dtype=np.float32)
    tk = np.arange(128)[:, None]
    u = np.arange(896)[None, :]
    masks = np.where(tk <= u - 384, 0.0, NEG).astype(np.float32)
    sel16 = np.zeros((NE, ER), dtype=np.float32)
    for e in range(NE):
        sel16[e, e * NR:(e + 1) * NR] = 1.0
    ones = np.ones((128, 128), dtype=np.float32)
    cv = np.zeros((128, 8), dtype=np.float32)
    cv[:, 0] = EPS
    cv[:, 1] = 1.0
    cv[:, 2] = -SHIFT
    cv[:, 3] = -0.5
    cv[:, 4] = -1.0
    cv[:, 5] = SCALE_QK
    return {
        "identd": ident, "identbd": ident, "masksd": masks,
        "cosd": cos, "sind": sin, "onesd": ones, "sel16d": sel16, "constsd": cv,
    }


def make_in_maps(inputs):
    x = np.asarray(inputs["L0_attn_a_x"])
    wq = np.asarray(inputs["L0_attn_b_Wq"])
    wk = np.asarray(inputs["L0_attn_b_Wk"])
    wv = np.asarray(inputs["L0_attn_b_Wv"])
    wo = np.asarray(inputs["L0_attn_b_Wo"])
    xpw = np.asarray(inputs["L0_attn_a_x_proj_w"])
    dtw = np.asarray(inputs["L0_attn_a_dt_proj_w"])
    dtb = np.asarray(inputs["L0_attn_a_dt_proj_b"])
    alog = np.asarray(inputs["L0_attn_a_A_log"])
    dv = np.asarray(inputs["L0_attn_a_D"])
    rw = np.asarray(inputs["L0_ffn_router_w"])
    fA = np.asarray(inputs["L0_ffn_A"])
    fB = np.asarray(inputs["L0_ffn_B"])

    shared = {
        "wqT": np.ascontiguousarray(wq.T), "wkT": np.ascontiguousarray(wk.T),
        "wvT": np.ascontiguousarray(wv.T), "woT": np.ascontiguousarray(wo.T),
        "xpwT": np.ascontiguousarray(xpw.T), "dtwT": np.ascontiguousarray(dtw.T),
        "dtbv": np.ascontiguousarray(dtb.reshape(D, 1)),
        "a_log": np.ascontiguousarray(alog),
        "dvec": np.ascontiguousarray(dv.reshape(D, 1)),
        "routT": np.ascontiguousarray(rw.T),
        "ffnAT": np.ascontiguousarray(fA.reshape(ER, D).T),
        "ffnB": np.ascontiguousarray(fB.transpose(0, 2, 1).reshape(ER, D)),
    }
    in_maps = []
    for cid in range(8):
        b = cid // 4
        lo, hi = _chunks_of_core(cid)
        xr = np.concatenate([x[b, lo * 512:(lo + 1) * 512, :],
                             x[b, hi * 512:(hi + 1) * 512, :]], axis=0)
        m = {"x_rows": np.ascontiguousarray(xr)}
        m.update(shared)
        m.update(_host_consts(cid))
        in_maps.append(m)
    return in_maps


def assemble_output(results):
    out = np.zeros((B, T, D), dtype=np.float32)
    for cid in range(8):
        b = cid // 4
        lo, hi = _chunks_of_core(cid)
        r = np.asarray(results[cid]["out_rows"])
        out[b, lo * 512:(lo + 1) * 512, :] = r[0:512]
        out[b, hi * 512:(hi + 1) * 512, :] = r[512:1024]
    return out


_CACHED = {}


def kernel(**inputs):
    key = "prog"
    if key not in _CACHED:
        nc, _ = build_program()
        _CACHED[key] = nc
    nc = _CACHED[key]
    in_maps = make_in_maps(inputs)
    res = run_bass_kernel_spmd(nc, in_maps, core_ids=list(range(8)))
    return assemble_output(res.results)



# version 08
# speedup vs baseline: 1.1395x; 1.1395x over previous
import math
import numpy as np
import ml_dtypes
import concourse.bass as bass
import concourse.mybir as mybir
from concourse.bass_utils import run_bass_kernel_spmd

F32 = mybir.dt.float32
BF16 = mybir.dt.bfloat16
AF = mybir.ActivationFunctionType
ALU = mybir.AluOpType
AXX = mybir.AxisListType.X

B, T, D = 2, 4096, 4096
NST = 16
NH, NKV, DH = 32, 8, 128
NE, NR = 4, 16
ER = NE * NR
TOKS = 1024
DSL = 512
SHIFT = 12.0
SCALE_QK = 1.0 / math.sqrt(DH)
LORA_SCALE = 4.0
EPS = 1e-5
NEG = -1.0e9
KVQ_ROWS = NKV * DH + TOKS + NH * DH


def _chunks_of_core(cid):
    q = cid % 4
    return (q, 7 - q)


def _tok_positions(cid):
    lo, hi = _chunks_of_core(cid)
    return np.concatenate([np.arange(lo * 512, lo * 512 + 512),
                           np.arange(hi * 512, hi * 512 + 512)])


def _llama3_cos_sin(positions):
    ROPE_THETA, ROPE_FACTOR, HIGH_FF, LOW_FF, ORIG = 500000.0, 8.0, 4.0, 1.0, 8192
    inv = 1.0 / ROPE_THETA ** (np.arange(0, DH, 2, dtype=np.float32) / DH)
    wavelen = 2.0 * np.pi / inv
    inv_scaled = np.where(wavelen > ORIG / LOW_FF, inv / ROPE_FACTOR, inv)
    smooth = (ORIG / wavelen - LOW_FF) / (HIGH_FF - LOW_FF)
    smoothed = (1.0 - smooth) / ROPE_FACTOR * inv + smooth * inv
    is_med = (wavelen <= ORIG / LOW_FF) & (wavelen >= ORIG / HIGH_FF)
    invf = np.where(is_med, smoothed, inv_scaled).astype(np.float32)
    freqs = positions.astype(np.float32)[None, :] * invf[:, None]
    emb = np.concatenate([freqs, freqs], axis=0)
    return np.cos(emb).astype(np.float32), np.sin(emb).astype(np.float32)


class Ctx:

    NPOOL = 96

    def __init__(self, nc):
        self.nc = nc
        self.sems = [nc.alloc_semaphore(f"sp{i}") for i in range(self.NPOOL)]
        self.scnt = [0] * self.NPOOL
        self.free_cmp = list(range(40))
        self.free_dma = list(range(40, 70))
        self.free_sw = list(range(70, self.NPOOL))
        self.kind = {}
        self.name2idx = {}
        self.base = {}
        self.history = []

    def _get(self, name, kind=0):
        if name not in self.name2idx:
            pool = (self.free_cmp, self.free_dma, self.free_sw)[kind]
            if not pool:
                raise RuntimeError("sem pool exhausted; add a barrier")
            idx = pool.pop(0)
            self.name2idx[name] = idx
            self.base[name] = self.scnt[idx]
            self.kind[name] = kind
            self.history.append((name, idx))
        return self.name2idx[name]

    def retire_all(self):
        for name, idx in self.name2idx.items():
            (self.free_cmp, self.free_dma, self.free_sw)[self.kind[name]].append(idx)
        self.name2idx = {}
        self.base = {}
        self.kind = {}

    def inc(self, name, inst, n=1):
        idx = self._get(name)
        self.scnt[idx] += n
        inst.then_inc(self.sems[idx], n)
        return self.scnt[idx] - self.base[name]

    def n(self, name):
        idx = self._get(name)
        return self.scnt[idx] - self.base[name]

    @staticmethod
    def _qkind(eng):
        return 2 if eng.engine == mybir.EngineType.Pool else 1

    def dma(self, eng, qname, out, in_):
        self._get(qname, kind=self._qkind(eng))
        inst = eng.dma_start(out=out, in_=in_)
        return self.inc(qname, inst, 16)

    def dmap(self, eng, base, idx, nbuf, out, in_):
        self._get(base + str(idx % nbuf), kind=self._qkind(eng))
        inst = eng.dma_start(out=out, in_=in_)
        return self.inc(base + str(idx % nbuf), inst, 16)

    def wdmap(self, eng, base, idx, nbuf, per_idx=1):
        if idx < 0:
            return
        self.wait(eng, base + str(idx % nbuf), (idx // nbuf + 1) * 16 * per_idx)

    def wait(self, eng, name, val):
        if val > 0:
            idx = self._get(name)
            eng.wait_ge(self.sems[idx], self.base[name] + val)

    def ser(self, eng, name, inst):
        v = self.inc(name, inst)
        idx = self.name2idx[name]
        eng.wait_ge(self.sems[idx], self.scnt[idx])
        return v


def build_program(dbg=()):
    nc = bass.Bass("TRN2", target_bir_lowering=False)
    c = Ctx(nc)

    x_rows = nc.dram_tensor("x_rows", [TOKS, D], F32, kind="ExternalInput").ap()
    out_rows = nc.dram_tensor("out_rows", [TOKS, D], F32, kind="ExternalOutput").ap()
    wqT = nc.dram_tensor("wqT", [D, NH * DH], F32, kind="ExternalInput").ap()
    wkT = nc.dram_tensor("wkT", [D, NKV * DH], F32, kind="ExternalInput").ap()
    wvT = nc.dram_tensor("wvT", [D, NKV * DH], F32, kind="ExternalInput").ap()
    woT = nc.dram_tensor("woT", [D, D], F32, kind="ExternalInput").ap()
    xpwT = nc.dram_tensor("xpwT", [D, 160], F32, kind="ExternalInput").ap()
    dtwT = nc.dram_tensor("dtwT", [128, D], F32, kind="ExternalInput").ap()
    dtbv = nc.dram_tensor("dtbv", [D, 1], F32, kind="ExternalInput").ap()
    a_log = nc.dram_tensor("a_log", [D, NST], F32, kind="ExternalInput").ap()
    dvec = nc.dram_tensor("dvec", [D, 1], F32, kind="ExternalInput").ap()
    routT = nc.dram_tensor("routT", [D, NE], F32, kind="ExternalInput").ap()
    ffnAT = nc.dram_tensor("ffnAT", [D, ER], F32, kind="ExternalInput").ap()
    ffnB = nc.dram_tensor("ffnB", [ER, D], F32, kind="ExternalInput").ap()
    identd = nc.dram_tensor("identd", [128, 128], F32, kind="ExternalInput").ap()
    identbd = nc.dram_tensor("identbd", [128, 128], F32, kind="ExternalInput").ap()
    masksd = nc.dram_tensor("masksd", [128, 896], F32, kind="ExternalInput").ap()
    cosd = nc.dram_tensor("cosd", [128, TOKS], F32, kind="ExternalInput").ap()
    sind = nc.dram_tensor("sind", [128, TOKS], F32, kind="ExternalInput").ap()
    onesd = nc.dram_tensor("onesd", [128, 128], F32, kind="ExternalInput").ap()
    sel16d = nc.dram_tensor("sel16d", [NE, ER], F32, kind="ExternalInput").ap()
    constsd = nc.dram_tensor("constsd", [128, 8], F32, kind="ExternalInput").ap()

    xT_dram = nc.dram_tensor("xT_dram", [D, TOKS], F32).ap()
    xn1_loc = nc.dram_tensor("xn1_loc", [D, TOKS], BF16).ap()
    xn1_ag = nc.dram_tensor("xn1_ag", [4 * D, TOKS], BF16).ap()
    xp_loc = nc.dram_tensor("xp_loc", [160, TOKS], BF16).ap()
    xp_ag = nc.dram_tensor("xp_ag", [4 * 160, TOKS], BF16).ap()
    y_loc = nc.dram_tensor("y_loc", [1024, T], BF16).ap()
    y_ag = nc.dram_tensor("y_ag", [D, T], BF16).ap()
    bc_mine = nc.dram_tensor("bc_mine", [2 * NST, T], BF16).ap()
    kvq_loc = nc.dram_tensor("kvq_loc", [KVQ_ROWS, TOKS], BF16).ap()
    kvq_ag = nc.dram_tensor("kvq_ag", [4 * KVQ_ROWS, TOKS], BF16).ap()
    attn_loc = nc.dram_tensor("attn_loc", [8 * DH, T], BF16).ap()
    attn_ag = nc.dram_tensor("attn_ag", [4 * 8 * DH, T], BF16).ap()
    xn1_mine = nc.dram_tensor("xn1_mine", [4 * 1024, TOKS], BF16).ap()
    y_mine = nc.dram_tensor("y_mine", [D, TOKS], BF16).ap()
    q_mine = nc.dram_tensor("q_mine", [4 * 1024, TOKS], BF16).ap()
    k_mine = nc.dram_tensor("k_mine", [4 * 256, TOKS], BF16).ap()
    v_mine = nc.dram_tensor("v_mine", [4 * 256, TOKS], BF16).ap()
    attn_mine = nc.dram_tensor("attn_mine", [NH * DH, TOKS], BF16).ap()
    alog_mine = nc.dram_tensor("alog_mine", [1024, NST], F32).ap()
    dtb_mine = nc.dram_tensor("dtb_mine", [1024, 1], F32).ap()
    dvec_mine = nc.dram_tensor("dvec_mine", [1024, 1], F32).ap()
    dtw_mine = nc.dram_tensor("dtw_mine", [128, 1024], F32).ap()

    dbg_specs = {
        "xT_dram": (xT_dram, [D, TOKS], F32),
        "xn1_loc": (xn1_loc, [D, TOKS], BF16),
        "xp_loc": (xp_loc, [160, TOKS], BF16),
        "xp_ag": (xp_ag, [4 * 160, TOKS], BF16),
        "y_loc": (y_loc, [1024, T], BF16),
        "x1": (xT_dram, [D, TOKS], F32),
        "x2": (xT_dram, [D, TOKS], F32),
        "x3": (xT_dram, [D, TOKS], F32),
        "x4": (xT_dram, [D, TOKS], F32),
        "kvq_loc": (kvq_loc, [KVQ_ROWS, TOKS], BF16),
        "q_mine": (q_mine, [4 * 1024, TOKS], BF16),
        "attn_loc": (attn_loc, [8 * DH, T], BF16),
    }
    dbg_aps = {}
    for nm in dbg:
        src, shp, dt_ = dbg_specs[nm]
        dbg_aps[nm] = (nc.dram_tensor("dbg_" + nm, shp, dt_, kind="ExternalOutput").ap(), src)

    PF = nc.alloc_sbuf_tensor("PF", [128, 6 * 4096], F32)
    PB = nc.alloc_sbuf_tensor("PB", [128, 10 * 4096], BF16)
    PB2 = nc.alloc_sbuf_tensor("PB2", [128, 8192], BF16)
    PS = nc.alloc_sbuf_tensor("PS", [128, 2048], F32)
    WS = nc.alloc_sbuf_tensor("WS", [128, 2560], BF16)
    identf_sb = nc.alloc_sbuf_tensor("identf_sb", [128, 128], F32)
    identb_sb = nc.alloc_sbuf_tensor("identb_sb", [128, 128], BF16)
    ones_bf = nc.alloc_sbuf_tensor("ones_bf", [128, 128], BF16)
    ones_f = nc.alloc_sbuf_tensor("ones_f", [128, 128], F32)
    sel16_sb = nc.alloc_sbuf_tensor("sel16_sb", [NE, ER], BF16)
    CONSTS = nc.alloc_sbuf_tensor("CONSTS", [128, 8], F32)

    PSUM = [nc.alloc_psum_tensor(f"psb{i}", [128, 512], F32) for i in range(8)]
    cos_sb = PB2[:, 4096:5120]
    sin_sb = PB2[:, 5120:6144]
    masks_sb = PB2[:, 6144:7040]

    def pf(i, w=4096):
        return PF[:, i * 4096:i * 4096 + w]

    def pb(i, w=4096):
        return PB[:, i * 4096:i * 4096 + w]

    blk = nc.Block()
    blk.__enter__()

    def barrier():
        snap = [(c.sems[idx], c.scnt[idx]) for idx in sorted(set(c.name2idx.values()))]
        for get in (blk.tensor, blk.scalar, blk.vector, blk.gpsimd, blk.sync):
            def body(eng, snap=snap):
                for s, v in snap:
                    if v > 0:
                        eng.wait_ge(s, v)
            get(body)
        c.retire_all()

    def emit_dbg(nm):
        if nm not in dbg_aps:
            return
        dst, src = dbg_aps[nm]
        barrier()

        def go(sy):
            rows = src.shape[0]
            r0 = 0
            while r0 < rows:
                rr = min(128, rows - r0)
                c.dma(sy, "qsync", dst[r0:r0 + rr, :], src[r0:r0 + rr, :])
                r0 += rr
        blk.sync(go)
        barrier()

    def p_consts(sy):
        c.dma(sy, "qsync", CONSTS[:], constsd)
        c.dma(sy, "qsync", identf_sb[:], identd)
        c.dma(sy, "qsync", ones_f[:], onesd)
        

    blk.sync(p_consts)

    def p_stage0(a):
        pid = a.partition_id()
        q = pid % 4
        c.dma(a, "qs", alog_mine, a_log[bass.ts(q, 1024), :])
        c.dma(a, "qs", dtb_mine, dtbv[bass.ts(q, 1024), :])
        c.dma(a, "qs", dvec_mine, dvec[bass.ts(q, 1024), :])
        c.dma(a, "qs", dtw_mine, dtwT[:, bass.ts(q, 1024)])
    blk.scalar(p_stage0)

    def p_consts_g(g):
        c.dma(g, "qg", ones_bf[:], onesd)
        c.dma(g, "qg", sel16_sb[:], sel16d)
        c.dma(g, "qg", identb_sb[:], identbd)
    blk.gpsimd(p_consts_g)
    barrier()

    def emit_stats_mm(t, tag, dg, xb_ap, sq_ap, last=31):
        for q2, src in ((0, xb_ap), (1, sq_ap)):
            for ntile in range(2):
                inst = t.matmul(PSUM[4 + 2 * q2 + ntile][0:1, :], ones_bf[:, 0:1],
                                src[:, ntile * 512:(ntile + 1) * 512],
                                start=(dg == 0), stop=(dg == last))
                c.inc(tag + "_mm", inst)

    def emit_ln_finalize(tag, stats_tag):
        def v_fin(v):
            c.wait(v, stats_tag + "_mm", c.n(stats_tag + "_mm"))
            v.tensor_scalar_mul(PS[0:1, 0:512], PSUM[4][0:1, :], 1.0 / D)
            v.tensor_scalar_mul(PS[0:1, 512:1024], PSUM[5][0:1, :], 1.0 / D)
            v.tensor_scalar_mul(PS[0:1, 1024:1536], PSUM[6][0:1, :], 1.0 / D)
            inst = v.tensor_scalar_mul(PS[0:1, 1536:2048], PSUM[7][0:1, :], 1.0 / D)
            c.ser(v, tag + "_ch", inst)
            v.tensor_mul(PSUM[4][0:1, :], PS[0:1, 0:512], PS[0:1, 0:512])
            inst = v.tensor_mul(PSUM[5][0:1, :], PS[0:1, 512:1024], PS[0:1, 512:1024])
            c.ser(v, tag + "_ch", inst)
            v.tensor_sub(PS[0:1, 1024:1536], PS[0:1, 1024:1536], PSUM[4][0:1, :])
            inst = v.tensor_sub(PS[0:1, 1536:2048], PS[0:1, 1536:2048], PSUM[5][0:1, :])
            c.inc(tag + "_var", inst)
        blk.vector(v_fin)

        def a_fin(a):
            c.wait(a, tag + "_var", c.n(tag + "_var"))
            inst = a.activation(out=PS[0:1, 1024:2048], in_=PS[0:1, 1024:2048], func=AF.Ln,
                                bias=CONSTS[0:1, 0:1])
            c.ser(a, tag + "_cha", inst)
            inst = a.activation(out=PS[0:1, 1024:2048], in_=PS[0:1, 1024:2048], func=AF.Exp,
                                scale=CONSTS[0:1, 3:4])
            c.inc(tag + "_fin", inst)
        blk.scalar(a_fin)

        def t_bc(t):
            c.wait(t, tag + "_fin", c.n(tag + "_fin"))
            for ntile in range(2):
                inst = t.matmul(PSUM[4 + ntile][:, :], ones_f[0:1, :],
                                PS[0:1, ntile * 512:(ntile + 1) * 512], start=True, stop=True)
                c.inc(tag + "_bc", inst)
            for ntile in range(2):
                inst = t.matmul(PSUM[6 + ntile][:, :], ones_f[0:1, :],
                                PS[0:1, 1024 + ntile * 512:1024 + (ntile + 1) * 512],
                                start=True, stop=True)
                c.inc(tag + "_bc", inst)
        blk.tensor(t_bc)

    def emit_xn_tile(v, tag, lane_src, lane_tmp, xn_dst):
        for ntile in range(2):
            sl = slice(ntile * 512, (ntile + 1) * 512)
            inst = v.tensor_sub(lane_tmp[:, sl], lane_src[:, sl], PSUM[4 + ntile][:, :])
            c.ser(v, tag + "_s", inst)
            inst = v.tensor_mul(xn_dst[:, sl], lane_tmp[:, sl], PSUM[6 + ntile][:, :])
            c.inc(tag, inst)

    def p0_sync(sy):
        for dg2 in range(0, 16, 2):
            if dg2 >= 2:
                c.wait(sy, "p0_tr", (dg2 - 1) * 16)
            src = x_rows[:, dg2 * 256:(dg2 + 1) * 256].rearrange("(a p) w -> p a w", p=128)
            c.dmap(sy, "p0_ld", dg2, 2, pf(dg2 % 2, 2048), src)
    blk.sync(p0_sync)



    def p0_stats_for(t, dg2s):
        c.wait(t, "p0_cast", (dg2s + 1) * 4)
        for ds in range(2):
            off = (dg2s % 2) * 4096 + ds * 2048
            emit_stats_mm(t, "p0_st", dg2s * 2 + ds,
                          PB2[:, off:off + 1024], PB2[:, off + 1024:off + 2048], last=31)

    def p0_tensor(t):
        for dg2 in range(16):
            c.wdmap(t, "p0_ld", dg2, 2)
            for j in range(16):
                g = dg2 * 16 + j
                if g >= 4:
                    c.wait(t, "p0_ev", g - 3)
                tt, ds = j // 2, j % 2
                src = pf(dg2 % 2, 2048)[:, tt * 256 + ds * 128: tt * 256 + ds * 128 + 128]
                inst = t.transpose(PSUM[g % 4][:, 0:128], src, identf_sb[:])
                c.inc("p0_tr", inst)
            if dg2 >= 1:
                p0_stats_for(t, dg2 - 1)
        p0_stats_for(t, 15)
    blk.tensor(p0_tensor)

    def p0_vector(v):
        for dg2 in range(16):
            stage = [pf(2 + 2 * (dg2 % 2), 1024), pf(3 + 2 * (dg2 % 2), 1024)]
            if dg2 >= 2:
                c.wait(v, "p0_cast", (dg2 - 2) * 4 + 4)
                c.wdmap(v, "p0_xst", dg2 - 2, 2, per_idx=2)
            for j in range(16):
                g = dg2 * 16 + j
                c.wait(v, "p0_tr", g + 1)
                tt, ds = j // 2, j % 2
                inst = v.tensor_copy(stage[ds][:, tt * 128:(tt + 1) * 128], PSUM[g % 4][:, 0:128])
                c.inc("p0_ev", inst)
    blk.vector(p0_vector)

    def p0_scalar(a):
        def oddload(d2):
            if d2 < 16:
                if d2 >= 2:
                    c.wait(a, "p0_tr", (d2 - 1) * 16)
                srcx = x_rows[:, d2 * 256:(d2 + 1) * 256].rearrange("(a p) w -> p a w", p=128)
                c.dmap(a, "p0_ld", d2, 2, pf(d2 % 2, 2048), srcx)
        oddload(1)
        for dg2 in range(16):
            if dg2 % 2 == 0:
                oddload(dg2 + 3)
            stage = [pf(2 + 2 * (dg2 % 2), 1024), pf(3 + 2 * (dg2 % 2), 1024)]
            c.wait(a, "p0_st_mm", max(0, (dg2 - 1) * 8))
            c.wait(a, "p0_ev", dg2 * 16 + 16)
            for ds in range(2):
                off = (dg2 % 2) * 4096 + ds * 2048
                inst = a.activation(out=PB2[:, off:off + 1024], in_=stage[ds], func=AF.Identity)
                c.inc("p0_cast", inst)
                inst = a.activation(out=PB2[:, off + 1024:off + 2048], in_=stage[ds], func=AF.Square)
                c.inc("p0_cast", inst)
                dg = dg2 * 2 + ds
                c.dmap(a, "p0_xst", dg2, 2, xT_dram[dg * 128:(dg + 1) * 128, :], stage[ds])
    blk.scalar(p0_scalar)

    emit_ln_finalize("ln1", "p0_st")
    barrier()

    def p0b_sync(sy):
        for dg in range(32):
            if dg >= 2:
                c.wait(sy, "p0b_xn", (dg - 2) * 2 + 2)
            c.dmap(sy, "p0b_ld", dg, 2, pf(dg % 2, 1024), xT_dram[dg * 128:(dg + 1) * 128, :])
    blk.sync(p0b_sync)

    def p0b_g(g):
        for dg in range(32):
            if dg >= 2:
                c.wait(g, "p0b_mm", (dg - 1) * 4)
            off = (dg % 2) * 256
            c.dmap(g, "p0b_w", dg, 2, WS[:, off:off + 160], xpwT[dg * 128:(dg + 1) * 128, :])
    blk.gpsimd(p0b_g)

    def p0b_vector(v):
        for dg in range(32):
            c.wdmap(v, "p0b_ld", dg, 2)
            if dg >= 2:
                c.wdmap(v, "p0b_xnst", dg - 2, 2)
                c.wait(v, "p0b_mm", (dg - 1) * 4)
            emit_xn_tile(v, "p0b_xn", pf(dg % 2, 1024), pf(2, 1024), pb(dg % 2, 1024))
    blk.vector(p0b_vector)

    def p0b_scalar(a):
        for dg in range(32):
            c.wait(a, "p0b_xn", dg * 2 + 2)
            c.dmap(a, "p0b_xnst", dg, 2, xn1_loc[dg * 128:(dg + 1) * 128, :], pb(dg % 2, 1024))
    blk.scalar(p0b_scalar)

    def p0b_tensor(t):
        for dg in range(32):
            c.wdmap(t, "p0b_w", dg, 2)
            c.wait(t, "p0b_xn", dg * 2 + 2)
            off = (dg % 2) * 256
            lhs1 = WS[:, off:off + 128]
            lhs2 = WS[:, off + 128:off + 160]
            xn = pb(dg % 2, 1024)
            for ntile in range(2):
                inst = t.matmul(PSUM[ntile][:, :], lhs1, xn[:, ntile * 512:(ntile + 1) * 512],
                                start=(dg == 0), stop=(dg == 31))
                c.inc("p0b_mm", inst)
            for ntile in range(2):
                inst = t.matmul(PSUM[2 + ntile][0:32, :], lhs2, xn[:, ntile * 512:(ntile + 1) * 512],
                                start=(dg == 0), stop=(dg == 31))
                c.inc("p0b_mm", inst)
    blk.tensor(p0b_tensor)

    def p0b_scalar2(a):
        c.wait(a, "p0b_mm", c.n("p0b_mm"))
        for ntile in range(2):
            inst = a.activation(out=pb(2, 1024)[:, ntile * 512:(ntile + 1) * 512],
                                in_=PSUM[ntile][:, :], func=AF.Identity)
            c.inc("p0b_ev", inst)
        for ntile in range(2):
            inst = a.activation(out=pb(3, 1024)[0:32, ntile * 512:(ntile + 1) * 512],
                                in_=PSUM[2 + ntile][0:32, :], func=AF.Identity)
            c.inc("p0b_ev", inst)
        c.wait(a, "p0b_ev", c.n("p0b_ev"))
        c.dma(a, "p0b_xpst", xp_loc[0:128, :], pb(2, 1024))
        c.dma(a, "p0b_xpst", xp_loc[128:160, :], pb(3, 1024)[0:32, :])
    blk.scalar(p0b_scalar2)

    emit_dbg("xT_dram")
    emit_dbg("xn1_loc")
    emit_dbg("xp_loc")

    barrier()

    GRPS = [[0, 1, 2, 3], [4, 5, 6, 7]]

    def e01(g):
        c.inc("ccxp", g.collective_compute("AllGather", ALU.bypass,
                                           replica_groups=GRPS,
                                           ins=[xp_loc.opt()], outs=[xp_ag.opt()]))
        c.inc("ccxn", g.collective_compute("AllGather", ALU.bypass,
                                           replica_groups=GRPS,
                                           ins=[xn1_loc.opt()], outs=[xn1_ag.opt()]))
        c.wait(g, "ccxp", c.n("ccxp"))
        for j in range(4):
            lo, hi = _chunks_of_core(j)
            for half, ch in ((0, lo), (1, hi)):
                src = xp_ag[j * 160 + 128: j * 160 + 160, half * 512:(half + 1) * 512]
                c.dma(g, "qgbc", bc_mine[:, ch * 512:(ch + 1) * 512], src)
        c.wait(g, "ccxn", c.n("ccxn"))
        pid = g.partition_id()
        q = pid % 4
        c.dma(g, "qg", xn1_mine.rearrange("(r d) t -> r d t", d=1024),
              xn1_ag.rearrange("(r d) t -> r d t", d=D)[:, bass.ts(q, 1024), :])
    blk.gpsimd(e01)
    QGBC_STAGED = c.n("qgbc")
    QG_XN1_STAGED = c.n("qg")
    emit_dbg("xp_ag")

    def p1_prolog_sync(sy):
        for i in range(8):
            c.dma(sy, "p1_pl", PS[:, i * 16:(i + 1) * 16], alog_mine[i * 128:(i + 1) * 128, :])
            c.dma(sy, "p1_pl", PS[:, 128 + i:129 + i], dtb_mine[i * 128:(i + 1) * 128, :])
            c.dma(sy, "p1_pl", PS[:, 136 + i:137 + i], dvec_mine[i * 128:(i + 1) * 128, :])
        c.dma(sy, "p1_pl", PS[:, 144:1168], dtw_mine)
    blk.sync(p1_prolog_sync)

    def p1_prolog_a(a):
        c.wait(a, "p1_pl", c.n("p1_pl"))
        inst = a.activation(out=PS[:, 0:128], in_=PS[:, 0:128], func=AF.Exp)
        c.inc("p1_pa", inst)
        inst = a.activation(out=WS[:, 1536:2560], in_=PS[:, 144:1168], func=AF.Identity)
        c.inc("p1_pa", inst)
    blk.scalar(p1_prolog_a)

    def p1_prolog_v(v):
        c.wait(v, "p1_pa", c.n("p1_pa"))
        inst = v.tensor_scalar_mul(PS[:, 0:128], PS[:, 0:128], -1.0)
        c.inc("p1_pv", inst)
    blk.vector(p1_prolog_v)

    TILES = list(range(8))

    def p1_sync(sy):
        c.wait(sy, "ccxp", c.n("ccxp"))
        for k, i in enumerate(TILES):
            for ch in range(8):
                j = ch if ch < 4 else 7 - ch
                half = 0 if ch < 4 else 1
                if k >= 1 or ch >= 2:
                    gchunk = k * 8 + ch
                    c.wait(sy, "p1_mm", gchunk - 1)
                src = xp_ag[j * 160:j * 160 + 128,
                            half * 512:(half + 1) * 512]
                c.dmap(sy, "p1_di", k * 8 + ch, 2,
                       WS[:, 256 + (ch % 2) * 512:256 + (ch % 2) * 512 + 512], src)
            if k == 0:
                c.wait(sy, "qg", QG_XN1_STAGED)
            if k >= 2:
                c.wait(sy, "p1_v", k - 1)
                c.wait(sy, "p1_yf", (k - 1) * 8)
            for j in range(4):
                lo, hi = _chunks_of_core(j)
                for half, ch in ((0, lo), (1, hi)):
                    src = xn1_mine[j * 1024 + i * 128:j * 1024 + (i + 1) * 128,
                                   half * 512:(half + 1) * 512]
                    c.dmap(sy, "p1_xn", k, 2, pb(k % 2)[:, ch * 512:(ch + 1) * 512], src)
    blk.sync(p1_sync)

    def p1_g(g):
        c.wait(g, "qgbc", QGBC_STAGED)
        for k, i in enumerate(TILES):
            for n in range(16):
                gn = k * 16 + n
                if gn >= 2:
                    c.wait(g, "p1_u", gn - 1)
                    c.wait(g, "p1_hc", gn - 1)
                c.dmap(g, "p1_bcB", gn, 2, pb(2 + n % 2),
                       bc_mine[n:n + 1, :].broadcast_to([128, T]))
                c.dmap(g, "p1_bcC", gn, 2, pb(4 + n % 2),
                       bc_mine[16 + n:17 + n, :].broadcast_to([128, T]))
    blk.gpsimd(p1_g)

    def p1_tensor(t):
        for k, i in enumerate(TILES):
            if k == 0:
                c.wait(t, "p1_pa", c.n("p1_pa"))
            off = 1536 + i * 128
            for ch in range(8):
                gchunk = k * 8 + ch
                c.wdmap(t, "p1_di", gchunk, 2)
                if k >= 1:
                    c.wait(t, "p1_yf", (k - 1) * 8 + ch + 1)
                inst = t.matmul(PSUM[ch][:, :], WS[:, off:off + 128],
                                WS[:, 256 + (ch % 2) * 512:256 + (ch % 2) * 512 + 512],
                                start=True, stop=True)
                c.inc("p1_mm", inst)
            for n in range(16):
                gn = k * 16 + n
                c.wait(t, "p1_hc", gn + 1)
                if n == 0:
                    c.wait(t, "p1_act3", (k * 8 + 8) * 3)
                    c.wait(t, "p1_spv", (k * 8 + 8) * 2)
                hc = PB2[:, (n % 2) * 4096:(n % 2) * 4096 + 4096]
                for ch in range(8):
                    inst = t.matmul(PSUM[ch][:, :], identb_sb[:],
                                    hc[:, ch * 512:(ch + 1) * 512],
                                    start=(n == 0), stop=(n == 15))
                    c.inc("p1_ymm", inst)
    blk.tensor(p1_tensor)

    def p1_a(a):
        c.wait(a, "p1_pl", c.n("p1_pl"))
        c.wait(a, "p1_pv", c.n("p1_pv"))
        for k, i in enumerate(TILES):
            if k >= 1:
                c.wait(a, "p1_spv", k * 16)
            for ch in range(8):
                gchunk = k * 8 + ch
                c.wait(a, "p1_mm", gchunk + 1)
                sl = pf(5)[:, ch * 512:(ch + 1) * 512]
                inst = a.activation(out=sl, in_=PSUM[ch][:, :], func=AF.Abs,
                                    bias=PS[:, 128 + i:129 + i])
                c.ser(a, "p1_act3", inst)
                inst = a.activation(out=sl, in_=sl, func=AF.Exp, scale=CONSTS[:, 4:5])
                c.ser(a, "p1_act3", inst)
                inst = a.activation(out=sl, in_=sl, func=AF.Ln, bias=CONSTS[:, 1:2])
                c.inc("p1_act3", inst)
            for n in range(16):
                gn = k * 16 + n
                if gn >= 2:
                    c.wait(a, "p1_scan", gn - 1)
                c.wait(a, "p1_spv", (k * 8 + 8) * 2)
                inst = a.activation(out=pf(1 + n % 2), in_=pf(0), func=AF.Exp,
                                    scale=PS[:, i * 16 + n:i * 16 + n + 1])
                c.inc("p1_dA", inst)
            c.wait(a, "p1_yf", (k + 1) * 8)
            c.dma(a, "p1_yst", y_loc[i * 128:(i + 1) * 128, :], pb(9))
    blk.scalar(p1_a)

    def p1_v(v):
        c.wait(v, "p1_pl", c.n("p1_pl"))
        for k, i in enumerate(TILES):
            for ch in range(8):
                gchunk = k * 8 + ch
                c.wait(v, "p1_mm", gchunk + 1)
                c.wait(v, "p1_act3", gchunk * 3 + 3)
                sl_dt = pf(0)[:, ch * 512:(ch + 1) * 512]
                sl_l = pf(5)[:, ch * 512:(ch + 1) * 512]
                inst = v.tensor_scalar(out=sl_dt, in0=PSUM[ch][:, :],
                                       scalar1=PS[:, 128 + i:129 + i], scalar2=0.0,
                                       op0=ALU.add, op1=ALU.max)
                c.ser(v, "p1_spv", inst)
                inst = v.tensor_add(sl_dt, sl_dt, sl_l)
                c.inc("p1_spv", inst)
            c.wait(v, "p1_spv", c.n("p1_spv"))
            c.wdmap(v, "p1_xn", k, 2, per_idx=8)
            inst = v.tensor_mul(pb(8), pf(0), pb(k % 2))
            c.ser(v, "p1_v", inst)
            hb = [pf(3), pf(4)]
            for n in range(16):
                gn = k * 16 + n
                c.wdmap(v, "p1_bcB", gn, 2)
                inst = v.tensor_mul(pb(6 + n % 2), pb(8), pb(2 + n % 2))
                c.ser(v, "p1_u", inst)
                c.wait(v, "p1_dA", gn + 1)
                if gn >= 2:
                    c.wait(v, "p1_hc", gn - 1)
                inst = v.tensor_tensor_scan(out=hb[n % 2], data0=pf(1 + n % 2),
                                            data1=pb(6 + n % 2), initial=0.0,
                                            op0=ALU.mult, op1=ALU.add)
                c.ser(v, "p1_scan", inst)
                c.wdmap(v, "p1_bcC", gn, 2)
                if gn >= 2:
                    c.wait(v, "p1_ymm", (gn - 2) * 8 + 8)
                inst = v.tensor_mul(PB2[:, (n % 2) * 4096:(n % 2) * 4096 + 4096],
                                    hb[n % 2], pb(4 + n % 2))
                c.inc("p1_hc", inst)
            if k >= 1:
                c.wait(v, "p1_yst", k * 16)
            for ch in range(8):
                c.wait(v, "p1_ymm", k * 128 + 15 * 8 + ch + 1)
                inst = v.scalar_tensor_tensor(
                    out=pb(9)[:, ch * 512:(ch + 1) * 512],
                    in0=pb(k % 2)[:, ch * 512:(ch + 1) * 512],
                    scalar=PS[:, 136 + i:137 + i], in1=PSUM[ch][:, :],
                    op0=ALU.mult, op1=ALU.add)
                c.inc("p1_yf", inst)
    blk.vector(p1_v)

    emit_dbg("y_loc")

    def e2(g):
        c.wait(g, "p1_yst", c.n("p1_yst"))
        c.inc("cc", g.collective_compute("AllGather", ALU.bypass,
                                         replica_groups=GRPS,
                                         ins=[y_loc.opt()], outs=[y_ag.opt()]))
        c.wait(g, "cc", c.n("cc"))
        pid = g.partition_id()
        ECH = [pid % 4, 7 - pid % 4]
        for half in range(2):
            c.dma(g, "qg", y_mine[:, half * 512:(half + 1) * 512],
                  y_ag[:, bass.ts(ECH[half], 512)])
        inst = g.engine_nop()
        c.inc("e2done", inst)
    blk.gpsimd(e2)
    QG_Y_STAGED = c.n("qg")

    def emit_resid_stats_scalar(tag, dg, stage, a, do_stats):
        if do_stats:
            off = (dg % 2) * 4096
            inst = a.activation(out=PB2[:, off:off + 1024], in_=stage, func=AF.Identity)
            c.inc(tag + "_cast", inst)
            inst = a.activation(out=PB2[:, off + 1024:off + 2048], in_=stage, func=AF.Square)
            c.inc(tag + "_cast", inst)
        c.dmap(a, tag + "_xst", dg, 2, xT_dram[dg * 128:(dg + 1) * 128, :], stage)

    def emit_resid_stats_tensor(tag):
        def r_t(t):
            for dg in range(32):
                c.wait(t, tag + "_cast", dg * 2 + 2)
                off = (dg % 2) * 4096
                emit_stats_mm(t, tag, dg, PB2[:, off:off + 1024],
                              PB2[:, off + 1024:off + 2048], last=31)
        blk.tensor(r_t)

    def p2a():
        tag = "p2a"

        def r_sync(sy):
            c.wait(sy, "p1_yst", c.n("p1_yst"))
            for dg in range(32):
                if dg >= 2:
                    c.wait(sy, tag + "_add", (dg - 2) * 2 + 2)
                c.dmap(sy, tag + "_ld", dg, 2, pf(dg % 2, 1024),
                       xT_dram[dg * 128:(dg + 1) * 128, :])
                if dg == 0:
                    c.wait(sy, "qg", QG_Y_STAGED)
                c.dmap(sy, tag + "_ld", dg, 2, pb(dg % 2, 1024),
                       y_mine[dg * 128:(dg + 1) * 128, :])
        blk.sync(r_sync)

        def r_v(v):
            for dg in range(32):
                c.wdmap(v, tag + "_ld", dg, 2, per_idx=2)
                if dg >= 2:
                    c.wait(v, tag + "_cast", (dg - 2) * 2 + 2)
                    c.wdmap(v, tag + "_xst", dg - 2, 2)
                for ntile in range(2):
                    sl = slice(ntile * 512, (ntile + 1) * 512)
                    inst = v.tensor_add(pf(2 + dg % 2, 1024)[:, sl], pf(dg % 2, 1024)[:, sl],
                                        pb(dg % 2, 1024)[:, sl])
                    c.inc(tag + "_add", inst)
        blk.vector(r_v)

        def r_a(a):
            c.wait(a, "p1_yst", c.n("p1_yst"))
            for dg in range(32):
                c.wait(a, tag + "_add", dg * 2 + 2)
                c.wait(a, tag + "_mm", max(0, (dg - 1) * 4))
                emit_resid_stats_scalar(tag, dg, pf(2 + dg % 2, 1024), a, True)
        blk.scalar(r_a)

        emit_resid_stats_tensor(tag)
    p2a()
    emit_ln_finalize("ln2", "p2a")
    barrier()
    emit_dbg("x1")

    def emit_ffn(tag, stats_next):
        def a_sync(sy):
            for dg in range(32):
                if dg >= 2:
                    c.wait(sy, tag + "_xn", (dg - 2) * 2 + 2)
                c.dmap(sy, tag + "_ld", dg, 2, pf(dg % 2, 1024),
                       xT_dram[dg * 128:(dg + 1) * 128, :])
        blk.sync(a_sync)

        def a_g(g):
            for dg in range(32):
                if dg >= 2:
                    c.wait(g, tag + "_amm", (dg - 1) * 4)
                off = (dg % 2) * 256
                c.dmap(g, tag + "_w", dg, 2, WS[:, off:off + 64],
                       ffnAT[dg * 128:(dg + 1) * 128, :])
                c.dmap(g, tag + "_w", dg, 2, WS[:, off + 64:off + 68],
                       routT[dg * 128:(dg + 1) * 128, :])
        blk.gpsimd(a_g)

        def a_v(v):
            for dg in range(32):
                c.wdmap(v, tag + "_ld", dg, 2)
                if dg >= 2:
                    c.wait(v, tag + "_amm", (dg - 1) * 4)
                emit_xn_tile(v, tag + "_xn", pf(dg % 2, 1024), pf(2, 1024), pb(dg % 2, 1024))
        blk.vector(a_v)

        def a_t(t):
            for dg in range(32):
                c.wdmap(t, tag + "_w", dg, 2, per_idx=2)
                c.wait(t, tag + "_xn", dg * 2 + 2)
                off = (dg % 2) * 256
                xn = pb(dg % 2, 1024)
                for ntile in range(2):
                    inst = t.matmul(PSUM[ntile][0:ER, :], WS[:, off:off + 64],
                                    xn[:, ntile * 512:(ntile + 1) * 512],
                                    start=(dg == 0), stop=(dg == 31))
                    c.inc(tag + "_amm", inst)
                for ntile in range(2):
                    inst = t.matmul(PSUM[2 + ntile][0:NE, :], WS[:, off + 64:off + 68],
                                    xn[:, ntile * 512:(ntile + 1) * 512],
                                    start=(dg == 0), stop=(dg == 31))
                    c.inc(tag + "_amm", inst)
        blk.tensor(a_t)

        def b_v1(v):
            c.wait(v, tag + "_amm", c.n(tag + "_amm"))
            v.tensor_copy(PS[0:4, 0:512], PSUM[2][0:4, :])
            inst = v.tensor_copy(PS[0:4, 512:1024], PSUM[3][0:4, :])
            c.inc(tag + "_lev", inst)
        blk.vector(b_v1)

        def b_t1(t):
            c.wait(t, tag + "_lev", c.n(tag + "_lev"))
            for tt in range(8):
                inst = t.transpose(PSUM[2][:, tt * 4:(tt + 1) * 4],
                                   PS[0:4, tt * 128:(tt + 1) * 128], identf_sb[0:4, 0:4])
                c.inc(tag + "_ltr", inst)
        blk.tensor(b_t1)

        def b_v2(v):
            c.wait(v, tag + "_ltr", c.n(tag + "_ltr"))
            lg = PS[:, 0:32].rearrange("p (a e) -> p a e", e=4)
            inst = v.tensor_copy(PS[:, 0:32], PSUM[2][:, 0:32])
            c.ser(v, tag + "_gc", inst)
            cnt = PS[:, 64:96].rearrange("p (a e) -> p a e", e=4)
            tmp = PS[:, 32:64].rearrange("p (a e) -> p a e", e=4)
            inst = v.memset(PS[:, 64:96], 0.0)
            c.ser(v, tag + "_gc", inst)
            for e2 in range(4):
                for e1 in range(4):
                    if e1 == e2:
                        continue
                    inst = v.tensor_tensor(out=tmp[:, :, e2], in0=lg[:, :, e1], in1=lg[:, :, e2],
                                           op=ALU.is_gt)
                    c.ser(v, tag + "_gc", inst)
                    inst = v.tensor_add(cnt[:, :, e2], cnt[:, :, e2], tmp[:, :, e2])
                    c.ser(v, tag + "_gc", inst)
            inst = v.tensor_single_scalar(PS[:, 64:96], PS[:, 64:96], 1.5, ALU.is_le)
            c.ser(v, tag + "_gc", inst)
            inst = v.reduce_max(PS[:, 96:104], lg, axis=AXX)
            c.ser(v, tag + "_gc", inst)
            inst = v.engine_nop()
            c.inc(tag + "_g1", inst)
        blk.vector(b_v2)


        def b_v3(v):
            lg = PS[:, 0:32].rearrange("p (a e) -> p a e", e=4)
            lm = PS[:, 32:64].rearrange("p (a e) -> p a e", e=4)
            for e in range(4):
                inst = v.tensor_tensor(out=lm[:, :, e], in0=lg[:, :, e], in1=PS[:, 96:104],
                                       op=ALU.subtract)
                c.ser(v, tag + "_gc", inst)
            inst = v.engine_nop()
            c.inc(tag + "_g2", inst)
        blk.vector(b_v3)

        def b_a2(a):
            c.wait(a, tag + "_g2", c.n(tag + "_g2"))
            inst = a.activation(out=PS[:, 32:64], in_=PS[:, 32:64], func=AF.Exp)
            c.inc(tag + "_g3", inst)
        blk.scalar(b_a2)

        def b_v4(v):
            c.wait(v, tag + "_g3", c.n(tag + "_g3"))
            inst = v.tensor_mul(PS[:, 32:64], PS[:, 32:64], PS[:, 64:96])
            c.ser(v, tag + "_gc", inst)
            ek = PS[:, 32:64].rearrange("p (a e) -> p a e", e=4)
            inst = v.reduce_sum(PS[:, 96:104], ek, axis=AXX)
            c.ser(v, tag + "_gc", inst)
            inst = v.reciprocal(PS[:, 104:112], PS[:, 96:104])
            c.ser(v, tag + "_gc", inst)
            for e in range(4):
                inst = v.tensor_tensor(out=ek[:, :, e], in0=ek[:, :, e], in1=PS[:, 104:112],
                                       op=ALU.mult)
                c.ser(v, tag + "_gc", inst)
            inst = v.tensor_single_scalar(PS[:, 32:64], PS[:, 32:64], LORA_SCALE, ALU.mult)
            c.inc(tag + "_g4", inst)
        blk.vector(b_v4)

        def b_t2(t):
            c.wait(t, tag + "_g4", c.n(tag + "_g4"))
            for tt in range(8):
                inst = t.transpose(PSUM[2 + tt // 4][0:4, (tt % 4) * 128:(tt % 4) * 128 + 128],
                                   PS[:, 32 + tt * 4:36 + tt * 4], identf_sb[:])
                c.inc(tag + "_gtr", inst)
        blk.tensor(b_t2)

        def b_v5(v):
            c.wait(v, tag + "_gtr", c.n(tag + "_gtr"))
            v.tensor_copy(pb(4, 1024)[0:4, 0:512], PSUM[2][0:4, :])
            inst = v.tensor_copy(pb(4, 1024)[0:4, 512:1024], PSUM[3][0:4, :])
            c.inc(tag + "_gev", inst)
        blk.vector(b_v5)

        def b_t3(t):
            c.wait(t, tag + "_gev", c.n(tag + "_gev"))
            for ntile in range(2):
                inst = t.matmul(PSUM[2 + ntile][0:ER, :], sel16_sb[:],
                                pb(4, 1024)[0:4, ntile * 512:(ntile + 1) * 512],
                                start=True, stop=True)
                c.inc(tag + "_grep", inst)
        blk.tensor(b_t3)

        def b_v6(v):
            c.wait(v, tag + "_grep", c.n(tag + "_grep"))
            v.tensor_copy(pb(2, 1024)[0:ER, 0:512], PSUM[2][0:ER, :])
            inst = v.tensor_copy(pb(2, 1024)[0:ER, 512:1024], PSUM[3][0:ER, :])
            c.ser(v, tag + "_gc", inst)
            for ntile in range(2):
                sl = slice(ntile * 512, (ntile + 1) * 512)
                inst = v.tensor_mul(pb(3, 1024)[0:ER, sl], PSUM[ntile][0:ER, :],
                                    pb(2, 1024)[0:ER, sl])
            c.inc(tag + "_midg", inst)
        blk.vector(b_v6)

        def c_sync(sy):
            c.wait(sy, tag + "_xn", 64)
            for dg in range(32):
                if dg >= 2:
                    c.wait(sy, tag + "_upd", (dg - 2) * 2 + 2)
                c.dmap(sy, tag + "_xld", dg, 2, pf(dg % 2, 1024),
                       xT_dram[dg * 128:(dg + 1) * 128, :])
        blk.sync(c_sync)

        def c_g(g):
            for dg in range(32):
                if dg >= 2:
                    c.wait(g, tag + "_dmm", (dg - 1) * 2)
                off = 512 + (dg % 2) * 128
                c.dmap(g, tag + "_bw", dg, 2, WS[0:ER, off:off + 128],
                       ffnB[:, dg * 128:(dg + 1) * 128])
        blk.gpsimd(c_g)

        def c_t(t):
            c.wait(t, tag + "_midg", c.n(tag + "_midg"))
            for dg in range(32):
                c.wdmap(t, tag + "_bw", dg, 2)
                if dg >= 2:
                    c.wait(t, tag + "_upd", (dg - 2) * 2 + 2)
                off = 512 + (dg % 2) * 128
                bp = (dg % 2) * 2
                for ntile in range(2):
                    inst = t.matmul(PSUM[bp + ntile][:, :], WS[0:ER, off:off + 128],
                                    pb(3, 1024)[0:ER, ntile * 512:(ntile + 1) * 512],
                                    start=True, stop=True)
                    c.inc(tag + "_dmm", inst)
                if stats_next and dg >= 1:
                    sdg = dg - 1
                    c.wait(t, tag + "_cast", (sdg + 1) * 2)
                    off2 = (sdg % 2) * 4096
                    emit_stats_mm(t, tag, sdg, PB2[:, off2:off2 + 1024],
                                  PB2[:, off2 + 1024:off2 + 2048], last=31)
            if stats_next:
                c.wait(t, tag + "_cast", 64)
                off2 = (31 % 2) * 4096
                emit_stats_mm(t, tag, 31, PB2[:, off2:off2 + 1024],
                              PB2[:, off2 + 1024:off2 + 2048], last=31)
        blk.tensor(c_t)

        def c_v(v):
            for dg in range(32):
                c.wdmap(v, tag + "_xld", dg, 2)
                c.wait(v, tag + "_dmm", dg * 2 + 2)
                if dg >= 2:
                    if stats_next:
                        c.wait(v, tag + "_cast", (dg - 2) * 2 + 2)
                    c.wdmap(v, tag + "_xst", dg - 2, 2)
                bp = (dg % 2) * 2
                for ntile in range(2):
                    sl = slice(ntile * 512, (ntile + 1) * 512)
                    inst = v.tensor_add(pf(2 + dg % 2, 1024)[:, sl], pf(dg % 2, 1024)[:, sl],
                                        PSUM[bp + ntile][:, :])
                    c.inc(tag + "_upd", inst)
        blk.vector(c_v)

        def c_a(a):
            for dg in range(32):
                c.wait(a, tag + "_upd", dg * 2 + 2)
                if stats_next:
                    c.wait(a, tag + "_mm", max(0, (dg - 1) * 4))
                emit_resid_stats_scalar(tag, dg, pf(2 + dg % 2, 1024), a, stats_next)
        blk.scalar(c_a)

    emit_ffn("f1", stats_next=True)
    emit_ln_finalize("ln3", "f1")
    emit_dbg("x2")

    barrier()

    def p3_pre_g(g):
        c.dma(g, "p3_cs", cos_sb, cosd)
        c.dma(g, "p3_cs", sin_sb, sind)
    blk.gpsimd(p3_pre_g)

    def p3_sync(sy):
        for dg in range(32):
            if dg >= 2:
                c.wait(sy, "p3_xn", (dg - 2) * 2 + 2)
            c.dmap(sy, "p3_ld", dg, 2, pf(dg % 2, 1024), xT_dram[dg * 128:(dg + 1) * 128, :])
    blk.sync(p3_sync)

    def p3_v1(v):
        for dg in range(32):
            c.wdmap(v, "p3_ld", dg, 2)
            emit_xn_tile(v, "p3_xn", pf(dg % 2, 1024), pf(2, 1024), PB[:, dg * 1024:(dg + 1) * 1024])
    blk.vector(p3_v1)

    NT_QKV = 48

    def p3_g(sy2):
        sy2
        c.wait(sy2, "p3_xn", 64)
        for ot in range(NT_QKV):
            if ot >= 2:
                c.wait(sy2, "p3_w", ot - 1)
            if ot < 32:
                w, col = wqT, ot * 128
            elif ot < 40:
                w, col = wkT, (ot - 32) * 128
            else:
                w, col = wvT, (ot - 40) * 128
            src = w[:, col:col + 128].rearrange("(a p) w -> p a w", p=128)
            c.dmap(sy2, "p3_wf", ot, 2, pf(ot % 2), src)
    blk.sync(p3_g)

    def p3_t(t):
        c.wait(t, "p3_xn", c.n("p3_xn"))
        for ot in range(NT_QKV):
            c.wait(t, "p3_w", ot + 1)
            bp = (ot % 2) * 2
            if ot >= 2:
                c.wait(t, "p3_ev", (ot - 1) * 2)
            band = pb(8 + ot % 2)
            for kc in range(32):
                for ntile in range(2):
                    inst = t.matmul(PSUM[bp + ntile][:, :], band[:, kc * 128:(kc + 1) * 128],
                                    PB[:, kc * 1024 + ntile * 512: kc * 1024 + ntile * 512 + 512],
                                    start=(kc == 0), stop=(kc == 31))
                    c.inc("p3_mm", inst)
    blk.tensor(p3_t)

    def p3_wc(v, ot):
        c.wdmap(v, "p3_wf", ot, 2)
        if ot >= 2:
            c.wait(v, "p3_mm", (ot - 1) * 64)
        inst = v.tensor_copy(pb(8 + ot % 2), pf(ot % 2))
        c.inc("p3_w", inst)

    def p3_v2(v):
        c.wait(v, "p3_cs", 32)
        p3_wc(v, 0)
        p3_wc(v, 1)
        for ot in range(NT_QKV):
            if ot + 2 < NT_QKV:
                p3_wc(v, ot + 2)
            c.wait(v, "p3_mm", (ot + 1) * 64)
            if ot >= 2:
                c.wdmap(v, "p3_kst", ot - 2, 2)
            bp = (ot % 2) * 2
            off = (ot % 2) * 2048
            outsl = PB2[:, off + 1024:off + 2048]
            if ot < 40:
                tmp = PB2[:, off:off + 1024]
                for ntile in range(2):
                    sl = slice(ntile * 512, (ntile + 1) * 512)
                    ps = PSUM[bp + ntile]
                    v.tensor_scalar_mul(tmp[0:64, sl], ps[64:128, :], -1.0)
                    inst = v.tensor_copy(tmp[64:128, sl], ps[0:64, :])
                    c.ser(v, "p3_rp", inst)
                    v.tensor_mul(pf(4, 1024)[:, sl], ps[:, :], cos_sb[:, sl])
                    inst = v.tensor_mul(pf(5, 1024)[:, sl], tmp[:, sl], sin_sb[:, sl])
                    c.ser(v, "p3_rp", inst)
                    inst = v.tensor_add(outsl[:, sl], pf(4, 1024)[:, sl], pf(5, 1024)[:, sl])
                    c.inc("p3_ev", inst)
            else:
                for ntile in range(2):
                    sl = slice(ntile * 512, (ntile + 1) * 512)
                    inst = v.tensor_copy(outsl[:, sl], PSUM[bp + ntile][:, :])
                    c.inc("p3_ev", inst)
    blk.vector(p3_v2)

    def p3_a(a):
        for ot in range(NT_QKV):
            c.wait(a, "p3_ev", (ot + 1) * 2)
            off = (ot % 2) * 2048
            outsl = PB2[:, off + 1024:off + 2048]
            if ot < 32:
                dst = kvq_loc[2048 + ot * 128: 2048 + (ot + 1) * 128, :]
            elif ot < 40:
                dst = kvq_loc[(ot - 32) * 128:(ot - 32) * 128 + 128, :]
            else:
                dst = xn1_loc[(ot - 40) * 128:(ot - 40) * 128 + 128, :]
            c.dmap(a, "p3_kst", ot, 2, dst, outsl)
    blk.scalar(p3_a)

    def p3_vt_sync(sy):
        c.wait(sy, "p3_kst0", c.n("p3_kst0"))
        c.wait(sy, "p3_kst1", c.n("p3_kst1"))
        for vt in range(8):
            if vt >= 2:
                c.wait(sy, "p3_vtr", (vt - 1) * 8)
            c.dmap(sy, "p3_vld", vt, 2, pb(8 + vt % 2, 1024),
                   xn1_loc[vt * 128:(vt + 1) * 128, :])
    blk.sync(p3_vt_sync)

    def p3_vt_t(t):
        for vt in range(8):
            c.wdmap(t, "p3_vld", vt, 2)
            for tt in range(8):
                g = vt * 8 + tt
                if g >= 4:
                    c.wait(t, "p3_vev", g - 3)
                inst = t.transpose(PSUM[g % 4][:, 0:128].bitcast(BF16)[:, 0:128],
                                   pb(8 + vt % 2, 1024)[:, tt * 128:(tt + 1) * 128],
                                   identb_sb[:])
                c.inc("p3_vtr", inst)
    blk.tensor(p3_vt_t)

    def p3_vt_v(v):
        for vt in range(8):
            for tt in range(8):
                g = vt * 8 + tt
                c.wait(v, "p3_vtr", g + 1)
                if g >= 2:
                    c.wdmap(v, "p3_vst", g - 2, 2)
                inst = v.tensor_copy(PB2[:, (g % 2) * 1024:(g % 2) * 1024 + 128],
                                     PSUM[g % 4][:, 0:128].bitcast(BF16)[:, 0:128])
                c.inc("p3_vev", inst)
    blk.vector(p3_vt_v)

    def p3_vt_a(a):
        for vt in range(8):
            for tt in range(8):
                g = vt * 8 + tt
                c.wait(a, "p3_vev", g + 1)
                dst = kvq_loc[1024 + vt * 128:1024 + (vt + 1) * 128, tt * 128:(tt + 1) * 128]
                c.dmap(a, "p3_vst", g, 2, dst, PB2[:, (g % 2) * 1024:(g % 2) * 1024 + 128])
    blk.scalar(p3_vt_a)

    emit_dbg("kvq_loc")

    def e3(g):
        c.wait(g, "p3_vst0", c.n("p3_vst0"))
        c.wait(g, "p3_vst1", c.n("p3_vst1"))
        c.wait(g, "p3_kst0", c.n("p3_kst0"))
        c.wait(g, "p3_kst1", c.n("p3_kst1"))
        c.inc("cc", g.collective_compute("AllGather", ALU.bypass,
                                         replica_groups=[[0, 1, 2, 3], [4, 5, 6, 7]],
                                         ins=[kvq_loc.opt()], outs=[kvq_ag.opt()]))
        c.wait(g, "cc", c.n("cc"))
        pid = g.partition_id()
        EJ = pid % 4
        kvqr = kvq_ag.rearrange("(r x) t -> r x t", x=KVQ_ROWS)
        c.dma(g, "qg", q_mine.rearrange("(r d) t -> r d t", d=1024),
              kvqr[:, 2048:6144, :][:, bass.ts(EJ, 1024), :])
        c.dma(g, "qg", k_mine.rearrange("(r d) t -> r d t", d=256),
              kvqr[:, 0:1024, :][:, bass.ts(EJ, 256), :])
        c.dma(g, "qg", v_mine.rearrange("(r d) t -> r d t", d=256),
              kvqr[:, 1024:2048, :][:, bass.ts(EJ, 256), :])
        inst = g.engine_nop()
        c.inc("e3done", inst)
    blk.gpsimd(e3)
    barrier()
    emit_dbg("q_mine")

    def nb_of(chunk):
        return 4 * chunk + 4

    def p4_pre_g(g):
        c.dma(g, "p4_mk", masks_sb, masksd)
    blk.gpsimd(p4_pre_g)

    def p4_sync(sy):
        for hh in range(8):
            if hh >= 2:
                c.wait(sy, "p4_done", (hh - 1) * 8)
            for rb in range(4):
                lo, hi = _chunks_of_core(rb)
                for half, ch in ((0, lo), (1, hi)):
                    src = q_mine[rb * 1024 + hh * 128:rb * 1024 + (hh + 1) * 128,
                                 half * 512:(half + 1) * 512]
                    c.dmap(sy, "p4_q", hh, 2, pb(hh % 2)[:, ch * 512:(ch + 1) * 512], src)
            for rb in range(4):
                lo, hi = _chunks_of_core(rb)
                for half, ch in ((0, lo), (1, hi)):
                    kh = hh // 4
                    src = k_mine[rb * 256 + kh * 128:rb * 256 + (kh + 1) * 128,
                                 half * 512:(half + 1) * 512]
                    c.dmap(sy, "p4_k", hh, 2, pb(2 + hh % 2)[:, ch * 512:(ch + 1) * 512], src)
            for rb in range(4):
                lo, hi = _chunks_of_core(rb)
                for half, ch in ((0, lo), (1, hi)):
                    kh = hh // 4
                    src = v_mine[rb * 256 + kh * 128:rb * 256 + (kh + 1) * 128,
                                 half * 512:(half + 1) * 512]
                    dst = pb(4 + hh % 2)[:, ch * 512:(ch + 1) * 512]
                    c.dmap(sy, "p4_v", hh, 2, dst, src)
    blk.sync(p4_sync)

    def p4_t(t):
        for hh in range(8):
            c.wdmap(t, "p4_q", hh, 2, per_idx=8)
            c.wdmap(t, "p4_k", hh, 2, per_idx=8)
            c.wdmap(t, "p4_v", hh, 2, per_idx=8)
            blocks = []
            for chunk in range(8):
                for tb in range(nb_of(chunk)):
                    blocks.append((chunk, tb))
            nblk = len(blocks)

            SB = [0, 1, 7]

            def s_mm(j):
                chunk, tb = blocks[j]
                gt = c.n("p4_smm")
                if gt >= 3:
                    c.wait(t, "p4_exp", gt - 2)
                if tb == 0 and hh * 8 + chunk >= 2:
                    c.wait(t, "p4_out", hh * 8 + chunk - 1)
                inst = t.matmul(PSUM[SB[gt % 3]][:, :], pb(2 + hh % 2)[:, tb * 128:(tb + 1) * 128],
                                pb(hh % 2)[:, chunk * 512:(chunk + 1) * 512],
                                start=True, stop=True)
                c.inc("p4_smm", inst)

            s_mm(0)
            if nblk > 1:
                s_mm(1)
            for j in range(nblk):
                if j + 2 < nblk:
                    s_mm(j + 2)
                chunk, tb = blocks[j]
                gt = hh * 144 + sum(nb_of(c2) for c2 in range(chunk)) + tb
                nb = nb_of(chunk)
                gc = hh * 8 + chunk
                c.wait(t, "p4_exp", gt + 1)
                pt = PB2[:, (gt % 3) * 512:(gt % 3) * 512 + 512]
                inst = t.matmul(PSUM[2 + chunk % 2][0:1, :], ones_bf[:, 0:1], pt,
                                start=(tb == 0), stop=(tb == nb - 1))
                c.inc("p4_lmm", inst)
                inst = t.matmul(PSUM[4 + chunk % 2][:, :],
                                pb(4 + hh % 2)[:, tb * 128:(tb + 1) * 128], pt,
                                start=(tb == 0), stop=(tb == nb - 1))
                c.inc("p4_pmm", inst)
                if tb == nb - 1:
                    c.wait(t, "p4_rcp", gc + 1)
                    c.wait(t, "p4_rev", gc)
                    inst = t.matmul(PSUM[6][:, :], ones_f[0:1, :],
                                    PS[0:1, (gc % 2) * 512:(gc % 2) * 512 + 512],
                                    start=True, stop=True)
                    c.inc("p4_rbc", inst)
    blk.tensor(p4_t)

    def p4_v_mask(v):
        c.wait(v, "p4_mk", 16)
        for hh in range(8):
            for chunk in range(8):
                nb = nb_of(chunk)
                gc = hh * 8 + chunk
                for tb in range(nb):
                    gt = hh * 144 + sum(nb_of(ch2) for ch2 in range(chunk)) + tb
                    if tb >= 4 * chunk:
                        toff = tb - 4 * chunk
                        c.wait(v, "p4_smm", gt + 1)
                        sbk = (0, 1, 7)[gt % 3]
                        inst = v.tensor_add(PSUM[sbk][:, :], PSUM[sbk][:, :],
                                            masks_sb[:, 384 - 128 * toff:896 - 128 * toff])
                        c.inc("p4_msk", inst)
                c.wait(v, "p4_lmm", c_lmm_at[gc])
                inst = v.reciprocal(PS[0:1, (gc % 2) * 512:(gc % 2) * 512 + 512],
                                    PSUM[2 + chunk % 2][0:1, :])
                c.inc("p4_rcp", inst)
                c.wait(v, "p4_rbc", gc + 1)
                inst = v.tensor_copy(PS[:, 1024 + (gc % 2) * 512:1024 + (gc % 2) * 512 + 512],
                                     PSUM[6][:, :])
                c.ser(v, "p4_rev", inst)
                c.wait(v, "p4_pmm", c_pmm_at[gc])
                if gc >= 2:
                    c.wdmap(v, "p4_ast", gc - 2, 2)
                inst = v.tensor_mul(PB2[:, 1536 + (gc % 2) * 512:1536 + (gc % 2) * 512 + 512],
                                    PSUM[4 + chunk % 2][:, :],
                                    PS[:, 1024 + (gc % 2) * 512:1024 + (gc % 2) * 512 + 512])
                c.inc("p4_out", inst)
    c_lmm_at = {}
    c_pmm_at = {}
    acc = 0
    for hh in range(8):
        for chunk in range(8):
            acc += nb_of(chunk)
            c_lmm_at[hh * 8 + chunk] = acc
            c_pmm_at[hh * 8 + chunk] = acc
    blk.vector(p4_v_mask)

    def p4_a(a):
        for hh in range(8):
            for chunk in range(8):
                nb = nb_of(chunk)
                gc = hh * 8 + chunk
                for tb in range(nb):
                    gt = hh * 144 + sum(nb_of(ch2) for ch2 in range(chunk)) + tb
                    c.wait(a, "p4_smm", gt + 1)
                    if tb >= 4 * chunk:
                        c.wait(a, "p4_msk", msk_count_at[gt])
                    pt = PB2[:, (gt % 3) * 512:(gt % 3) * 512 + 512]
                    if gt >= 3:
                        c.wait(a, "p4_lmm", gt - 2)
                        c.wait(a, "p4_pmm", gt - 2)
                    inst = a.activation(out=pt, in_=PSUM[(0, 1, 7)[gt % 3]][:, :], func=AF.Exp,
                                        scale=CONSTS[:, 5:6], bias=CONSTS[:, 2:3])
                    c.inc("p4_exp", inst)
                c.wait(a, "p4_out", gc + 1)
                dst = attn_loc[hh * 128:(hh + 1) * 128, chunk * 512:(chunk + 1) * 512]
                c._get("p4_ast" + str(gc % 2), kind=1)
                inst = a.dma_start(out=dst,
                                   in_=PB2[:, 1536 + (gc % 2) * 512:1536 + (gc % 2) * 512 + 512])
                c.inc("p4_ast" + str(gc % 2), inst, 16)
                inst = a.nop()
                c.inc("p4_done", inst)
    msk_count_at = {}
    mc = 0
    for hh in range(8):
        for chunk in range(8):
            nb = nb_of(chunk)
            for tb in range(nb):
                gt = hh * 144 + sum(nb_of(ch2) for ch2 in range(chunk)) + tb
                if tb >= 4 * chunk:
                    mc += 1
                    msk_count_at[gt] = mc
    blk.scalar(p4_a)

    emit_dbg("attn_loc")

    def e4(g):
        c.wait(g, "p4_ast0", c.n("p4_ast0"))
        c.wait(g, "p4_ast1", c.n("p4_ast1"))
        c.inc("cc", g.collective_compute("AllGather", ALU.bypass,
                                         replica_groups=[[0, 1, 2, 3], [4, 5, 6, 7]],
                                         ins=[attn_loc.opt()], outs=[attn_ag.opt()]))
        c.wait(g, "cc", c.n("cc"))
        inst = g.engine_nop()
        c.inc("e4done", inst)
    blk.gpsimd(e4)

    def p5_sync1(sy):
        for ht in range(2):
            c.dmap(sy, "p5_xld", ht, 2, pf(ht % 2, 1024), xT_dram[ht * 128:(ht + 1) * 128, :])
            srcw = woT[:, ht * 128:(ht + 1) * 128].rearrange("(a p) w -> p a w", p=128)
            c.dmap(sy, "p5_wf", ht, 2, pf(4 + ht % 2), srcw)
    blk.sync(p5_sync1)

    def p5_sync0(sy):
        c.wait(sy, "cc", c.n("cc"))
        pid = sy.partition_id()
        ECH = [pid % 4, 7 - pid % 4]
        for half in range(2):
            c.dma(sy, "p5_stg", attn_mine[:, half * 512:(half + 1) * 512],
                  attn_ag[:, bass.ts(ECH[half], 512)])
        c.wait(sy, "p5_stg", c.n("p5_stg"))
        for oc in range(32):
            c.dma(sy, "p5_ald", PB[:, oc * 1024:(oc + 1) * 1024],
                  attn_mine[oc * 128:(oc + 1) * 128, :])
        for ht in range(2, 32):
            c.wait(sy, "p5_upd", (ht - 2) * 2 + 2)
            c.wait(sy, "p5_w", ht - 1)
            c.dmap(sy, "p5_xld", ht, 2, pf(ht % 2, 1024), xT_dram[ht * 128:(ht + 1) * 128, :])
            srcw = woT[:, ht * 128:(ht + 1) * 128].rearrange("(a p) w -> p a w", p=128)
            c.dmap(sy, "p5_wf", ht, 2, pf(4 + ht % 2), srcw)
    blk.sync(p5_sync0)

    def p5_t(t):
        c.wait(t, "p5_ald", c.n("p5_ald"))
        for ht in range(32):
            c.wait(t, "p5_w", ht + 1)
            bp = (ht % 2) * 2
            if ht >= 2:
                c.wait(t, "p5_upd", (ht - 2) * 2 + 2)
            band = pb(8 + ht % 2)
            for kc in range(32):
                for ntile in range(2):
                    inst = t.matmul(PSUM[bp + ntile][:, :], band[:, kc * 128:(kc + 1) * 128],
                                    PB[:, kc * 1024 + ntile * 512:kc * 1024 + ntile * 512 + 512],
                                    start=(kc == 0), stop=(kc == 31))
                    c.inc("p5_mm", inst)
            if ht >= 1:
                sdg = ht - 1
                c.wait(t, "p5a_cast", (sdg + 1) * 2)
                off2 = (sdg % 2) * 4096
                emit_stats_mm(t, "p5a", sdg, PB2[:, off2:off2 + 1024],
                              PB2[:, off2 + 1024:off2 + 2048], last=31)
        c.wait(t, "p5a_cast", 64)
        off2 = (31 % 2) * 4096
        emit_stats_mm(t, "p5a", 31, PB2[:, off2:off2 + 1024],
                      PB2[:, off2 + 1024:off2 + 2048], last=31)
    blk.tensor(p5_t)

    def p5_wc(v, ht):
        c.wdmap(v, "p5_wf", ht, 2)
        if ht >= 2:
            c.wait(v, "p5_mm", (ht - 1) * 64)
        inst = v.tensor_copy(pb(8 + ht % 2), pf(4 + ht % 2))
        c.inc("p5_w", inst)

    def p5_v(v):
        p5_wc(v, 0)
        p5_wc(v, 1)
        for ht in range(32):
            c.wdmap(v, "p5_xld", ht, 2)
            c.wait(v, "p5_mm", (ht + 1) * 64)
            if ht >= 2:
                c.wait(v, "p5a_cast", (ht - 2) * 2 + 2)
                c.wdmap(v, "p5a_xst", ht - 2, 2)
            bp = (ht % 2) * 2
            for ntile in range(2):
                sl = slice(ntile * 512, (ntile + 1) * 512)
                inst = v.tensor_add(pf(2 + ht % 2, 1024)[:, sl], pf(ht % 2, 1024)[:, sl],
                                    PSUM[bp + ntile][:, :])
                c.inc("p5_upd", inst)
            if ht + 2 < 32:
                p5_wc(v, ht + 2)
    blk.vector(p5_v)

    def p5_a(a):
        for ht in range(32):
            c.wait(a, "p5_upd", ht * 2 + 2)
            c.wait(a, "p5a_mm", max(0, (ht - 1) * 4))
            emit_resid_stats_scalar("p5a", ht, pf(2 + ht % 2, 1024), a, True)
    blk.scalar(p5_a)
    emit_ln_finalize("ln4", "p5a")
    barrier()
    emit_dbg("x3")

    emit_ffn("f2", stats_next=False)
    emit_dbg("x4")

    barrier()

    def p6_sync(sy):
        for tt in range(8):
            for dg in range(32):
                g = tt * 32 + dg
                if g >= 8:
                    c.wait(sy, "p6_tr", g - 7)
                src = xT_dram[dg * 128:(dg + 1) * 128, tt * 128:(tt + 1) * 128]
                c.dmap(sy, "p6_ld", g, 8, pf(0, 1024)[:, (g % 8) * 128:(g % 8) * 128 + 128], src)
            c.wait(sy, "p6_ev", (tt + 1) * 32)
            c.dma(sy, "p6_st", out_rows[tt * 128:(tt + 1) * 128, :], pf(2))
        c.wait(sy, "p6_st", c.n("p6_st"))
    blk.sync(p6_sync)

    def p6_t(t):
        for tt in range(8):
            for dg in range(32):
                g = tt * 32 + dg
                c.wdmap(t, "p6_ld", g, 8)
                if g >= 4:
                    c.wait(t, "p6_ev", g - 3)
                inst = t.transpose(PSUM[g % 4][:, 0:128],
                                   pf(0, 1024)[:, (g % 8) * 128:(g % 8) * 128 + 128],
                                   identf_sb[:])
                c.inc("p6_tr", inst)
    blk.tensor(p6_t)

    def p6_v(v):
        for tt in range(8):
            if tt >= 1:
                c.wait(v, "p6_st", tt * 16)
            for dg in range(32):
                g = tt * 32 + dg
                c.wait(v, "p6_tr", g + 1)
                inst = v.tensor_copy(pf(2)[:, dg * 128:(dg + 1) * 128], PSUM[g % 4][:, 0:128])
                c.inc("p6_ev", inst)
    blk.vector(p6_v)

    barrier()
    blk.__exit__(None, None, None)
    nc._kctx = c
    return nc, [nm for nm in dbg]


def _host_consts(cid):
    pos = _tok_positions(cid)
    cos, sin = _llama3_cos_sin(pos)
    ident = np.eye(128, dtype=np.float32)
    tk = np.arange(128)[:, None]
    u = np.arange(896)[None, :]
    masks = np.where(tk <= u - 384, 0.0, NEG).astype(np.float32)
    sel16 = np.zeros((NE, ER), dtype=np.float32)
    for e in range(NE):
        sel16[e, e * NR:(e + 1) * NR] = 1.0
    ones = np.ones((128, 128), dtype=np.float32)
    cv = np.zeros((128, 8), dtype=np.float32)
    cv[:, 0] = EPS
    cv[:, 1] = 1.0
    cv[:, 2] = -SHIFT
    cv[:, 3] = -0.5
    cv[:, 4] = -1.0
    cv[:, 5] = SCALE_QK
    return {
        "identd": ident, "identbd": ident, "masksd": masks,
        "cosd": cos, "sind": sin, "onesd": ones, "sel16d": sel16, "constsd": cv,
    }


def make_in_maps(inputs):
    x = np.asarray(inputs["L0_attn_a_x"])
    wq = np.asarray(inputs["L0_attn_b_Wq"])
    wk = np.asarray(inputs["L0_attn_b_Wk"])
    wv = np.asarray(inputs["L0_attn_b_Wv"])
    wo = np.asarray(inputs["L0_attn_b_Wo"])
    xpw = np.asarray(inputs["L0_attn_a_x_proj_w"])
    dtw = np.asarray(inputs["L0_attn_a_dt_proj_w"])
    dtb = np.asarray(inputs["L0_attn_a_dt_proj_b"])
    alog = np.asarray(inputs["L0_attn_a_A_log"])
    dv = np.asarray(inputs["L0_attn_a_D"])
    rw = np.asarray(inputs["L0_ffn_router_w"])
    fA = np.asarray(inputs["L0_ffn_A"])
    fB = np.asarray(inputs["L0_ffn_B"])

    shared = {
        "wqT": np.ascontiguousarray(wq.T), "wkT": np.ascontiguousarray(wk.T),
        "wvT": np.ascontiguousarray(wv.T), "woT": np.ascontiguousarray(wo.T),
        "xpwT": np.ascontiguousarray(xpw.T), "dtwT": np.ascontiguousarray(dtw.T),
        "dtbv": np.ascontiguousarray(dtb.reshape(D, 1)),
        "a_log": np.ascontiguousarray(alog),
        "dvec": np.ascontiguousarray(dv.reshape(D, 1)),
        "routT": np.ascontiguousarray(rw.T),
        "ffnAT": np.ascontiguousarray(fA.reshape(ER, D).T),
        "ffnB": np.ascontiguousarray(fB.transpose(0, 2, 1).reshape(ER, D)),
    }
    in_maps = []
    for cid in range(8):
        b = cid // 4
        lo, hi = _chunks_of_core(cid)
        xr = np.concatenate([x[b, lo * 512:(lo + 1) * 512, :],
                             x[b, hi * 512:(hi + 1) * 512, :]], axis=0)
        m = {"x_rows": np.ascontiguousarray(xr)}
        m.update(shared)
        m.update(_host_consts(cid))
        in_maps.append(m)
    return in_maps


def assemble_output(results):
    out = np.zeros((B, T, D), dtype=np.float32)
    for cid in range(8):
        b = cid // 4
        lo, hi = _chunks_of_core(cid)
        r = np.asarray(results[cid]["out_rows"])
        out[b, lo * 512:(lo + 1) * 512, :] = r[0:512]
        out[b, hi * 512:(hi + 1) * 512, :] = r[512:1024]
    return out


_CACHED = {}


def kernel(**inputs):
    key = "prog"
    if key not in _CACHED:
        nc, _ = build_program()
        _CACHED[key] = nc
    nc = _CACHED[key]
    in_maps = make_in_maps(inputs)
    res = run_bass_kernel_spmd(nc, in_maps, core_ids=list(range(8)))
    return assemble_output(res.results)

